# version 14
# baseline (speedup 1.0000x reference)
import numpy as np
import ml_dtypes
from contextlib import ExitStack
import concourse.bass as bass
import concourse.mybir as mybir
from concourse.bass_utils import run_bass_kernel_spmd

F32 = mybir.dt.float32
BF16 = mybir.dt.bfloat16
AF = mybir.ActivationFunctionType
ALU = mybir.AluOpType
AX = mybir.AxisListType

D = 1024
S = 2048
NT = 16
KC = 8
EPS = 1e-5
NEPS = 1e-9
ISQ = 1.0 / 32.0


def build_nc(debug=False):
    nc = bass.Bass("TRN2", target_bir_lowering=False, detect_race_conditions=False)

    x_hbm = nc.dram_tensor("x", [S, D], F32, kind="ExternalInput").ap()
    wq_hbm = nc.dram_tensor("wq_t", [KC, 128, D], F32, kind="ExternalInput").ap()
    am_hbm = nc.dram_tensor("amat_t", [KC, 128, D], F32, kind="ExternalInput").ap()
    wv_hbm = nc.dram_tensor("wv_t", [KC, 128, D], F32, kind="ExternalInput").ap()
    bq_hbm = nc.dram_tensor("bq_t", [128, KC], F32, kind="ExternalInput").ap()
    bv_hbm = nc.dram_tensor("bv_row", [1, D], F32, kind="ExternalInput").ap()
    bf_hbm = nc.dram_tensor("brbi_row", [1, D], F32, kind="ExternalInput").ap()
    on_hbm = nc.dram_tensor("ones_row", [1, 128], F32, kind="ExternalInput").ap()
    id_hbm = nc.dram_tensor("ident", [128, 128], BF16, kind="ExternalInput").ap()
    e1_hbm = nc.dram_tensor("e1", [2, 4, 512, 512], F32, kind="ExternalInput").ap()
    e2_hbm = nc.dram_tensor("e2", [2, 4, 128, 512], F32, kind="ExternalInput").ap()

    y_hbm = nc.dram_tensor("y_out", [S, D], F32, kind="ExternalOutput").ap()
    if debug:
        dbg = {
            "d_qf": nc.dram_tensor("d_qf", [128, KC * S], BF16, kind="ExternalOutput").ap(),
            "d_tqf": nc.dram_tensor("d_tqf", [128, KC * S], BF16, kind="ExternalOutput").ap(),
            "d_vt": nc.dram_tensor("d_vt", [128, NT * D], BF16, kind="ExternalOutput").ap(),
            "d_wbig": nc.dram_tensor("d_wbig", [128, KC * D], BF16, kind="ExternalOutput").ap(),
            "d_norm": nc.dram_tensor("d_norm", [128, NT], F32, kind="ExternalOutput").ap(),
            "d_sums": nc.dram_tensor("d_sums", [128, NT], F32, kind="ExternalOutput").ap(),
        }

    es = ExitStack()

    qf = es.enter_context(nc.sbuf_tensor("qf", [128, KC * S], BF16))
    tqf = es.enter_context(nc.sbuf_tensor("tqf", [128, KC * S], BF16))
    vt = es.enter_context(nc.sbuf_tensor("vt", [128, NT * D], BF16))
    wbig = es.enter_context(nc.sbuf_tensor("wbig", [128, KC * D], BF16))
    x_t = [es.enter_context(nc.sbuf_tensor(f"x_t{i}", [128, D], F32)) for i in range(4)]
    bq_t = es.enter_context(nc.sbuf_tensor("sb_bq", [128, KC], F32))
    ident = es.enter_context(nc.sbuf_tensor("sb_id", [128, 128], BF16))
    bv_bc = es.enter_context(nc.sbuf_tensor("bv_bc", [128, D], F32))
    bf_bc = es.enter_context(nc.sbuf_tensor("bf_bc", [128, D], F32))
    norm_all = es.enter_context(nc.sbuf_tensor("norm_all", [128, NT], F32))
    negm_all = es.enter_context(nc.sbuf_tensor("negm_all", [128, NT], F32))
    sume_all = es.enter_context(nc.sbuf_tensor("sume_all", [128, NT], F32))
    bn12 = es.enter_context(nc.sbuf_tensor("bn12", [128, 12], F32))
    mv_all = es.enter_context(nc.sbuf_tensor("mv_all", [128, 2 * NT], F32))
    inv16 = es.enter_context(nc.sbuf_tensor("inv16", [128, NT], F32))
    sa16 = es.enter_context(nc.sbuf_tensor("sa16", [128, NT], F32))
    k16 = es.enter_context(nc.sbuf_tensor("k16", [128, NT], F32))
    vtmp = es.enter_context(nc.sbuf_tensor("vtmp", [128, NT], F32))
    epsc = es.enter_context(nc.sbuf_tensor("epsc", [128, 1], F32))
    rn16 = es.enter_context(nc.sbuf_tensor("rn16", [128, NT], F32))
    lns16 = es.enter_context(nc.sbuf_tensor("lns16", [128, NT], F32))
    lns = es.enter_context(nc.sbuf_tensor("lns", [128, 1], F32))

    es1 = ExitStack()
    scf = es1.enter_context(nc.sbuf_tensor("scf", [128, KC * S], BF16))
    sctok = [es1.enter_context(nc.sbuf_tensor("sctok0", [128, D], BF16))] * 2
    es1r = ExitStack()
    bv_row = es1r.enter_context(nc.sbuf_tensor("sb_bvr", [1, D], F32))
    bf_row = es1r.enter_context(nc.sbuf_tensor("bf_row", [1, D], F32))
    ones_r = es1r.enter_context(nc.sbuf_tensor("ones_r", [1, 128], F32))
    es1r.close()
    wv_f32 = [es1.enter_context(nc.sbuf_tensor(f"wv_f32_{i}", [128, D], F32)) for i in range(2)]
    es1a = ExitStack()
    e1_t = [es1a.enter_context(nc.sbuf_tensor(f"e1_t{i}", [128, 512], F32)) for i in range(2)]
    e2_sb = es1a.enter_context(nc.sbuf_tensor("e2_sb", [128, 2048], F32))
    wacc = es1a.enter_context(nc.sbuf_tensor("wacc", [128, 512], F32))
    wtmp = es1a.enter_context(nc.sbuf_tensor("wtmp", [128, 512], F32))
    w_f32 = [es1a.enter_context(nc.sbuf_tensor("w_f32_0", [128, D], F32))] * 2
    w_bf = [es1a.enter_context(nc.sbuf_tensor(f"w_bf_{i}", [128, D], BF16)) for i in range(2)]
    es1a.close()
    es1b = ExitStack()
    wv_bf = es1b.enter_context(nc.sbuf_tensor("wv_bf", [128, KC * D], BF16))
    es1b.close()
    es1.close()

    es2 = ExitStack()
    sc_sb = [es2.enter_context(nc.sbuf_tensor(f"sc_sb{i}", [128, S], F32)) for i in range(2)]
    probs = [es2.enter_context(nc.sbuf_tensor(f"probs{i}", [128, S], BF16)) for i in range(2)]
    probsT = es2.enter_context(nc.sbuf_tensor("probsT", [128, S], BF16))
    attn = es2.enter_context(nc.sbuf_tensor("attn", [128, D], F32))
    y_buf = [es2.enter_context(nc.sbuf_tensor(f"y_buf{i}", [128, D], F32)) for i in range(3)]
    ln2tok = [es2.enter_context(nc.sbuf_tensor(f"ln2tok{i}", [128, D], BF16)) for i in range(2)]
    ln2F = [es2.enter_context(nc.sbuf_tensor(f"ln2F{i}", [128, D], BF16)) for i in range(2)]
    ffn_f = es2.enter_context(nc.sbuf_tensor("ffn_f", [128, D], F32))
    out_b = [es2.enter_context(nc.sbuf_tensor(f"out_b{i}", [128, D], F32)) for i in range(2)]
    mvA = es2.enter_context(nc.sbuf_tensor("mvA", [128, 2], F32))
    mvL = es2.enter_context(nc.sbuf_tensor("mvL", [128, 2], F32))
    mvF = es2.enter_context(nc.sbuf_tensor("mvF", [128, 2], F32))
    invA = es2.enter_context(nc.sbuf_tensor("invA", [128, 1], F32))
    invL = es2.enter_context(nc.sbuf_tensor("invL", [128, 1], F32))
    invF = es2.enter_context(nc.sbuf_tensor("invF", [128, 1], F32))
    rs_s = es2.enter_context(nc.sbuf_tensor("rs_s", [128, 1], F32))
    csc = es2.enter_context(nc.sbuf_tensor("csc", [128, 1], F32))
    ytmp = es2.enter_context(nc.sbuf_tensor("ytmp", [128, D], F32))
    es2.close()

    esp1 = ExitStack()
    psq = [esp1.enter_context(nc.psum_tensor(f"psq{i}", [128, 512], F32)) for i in range(4)]
    psv = [esp1.enter_context(nc.psum_tensor(f"psv{i}", [128, 1024], F32)) for i in range(2)]
    esp1.close()
    esp2 = ExitStack()
    psA = esp2.enter_context(nc.psum_tensor("psA", [128, 512], F32))
    psB = esp2.enter_context(nc.psum_tensor("psB", [128, 512], F32))
    psC = esp2.enter_context(nc.psum_tensor("psC", [128, 512], F32))
    psD = esp2.enter_context(nc.psum_tensor("psD", [128, 512], F32))
    psE = esp2.enter_context(nc.psum_tensor("psE", [128, 1024], F32))
    psG = esp2.enter_context(nc.psum_tensor("psG", [128, 1024], F32))
    esp2.close()
    psAB = [psA, psB]
    psCD = [psC, psD]

    semnames = ("ld_c ld_x0 ld_x1 ld_x2 ld_x3 ld_e1a ld_e1b ld_e2 ld_w0 ld_w1 ld_wv st0 st1 t_bc v_bc v_r0 s_inv0 v_stat v_emit t_scT s_scF "
                "v_cast t_q s_qf v_vt v_wbm v_wb dbg v_g s_g t_sc v_scev v_max s_exp t_pT s_pT "
                "t_lT s_lF v_ln2e t_ctx v_ctxev t_ffn v_ffnev v_y v_out v_rA s_invA v_rL "
                "s_invL v_rF s_invF").split()
    sem = {n: es.enter_context(nc.semaphore(n)) for n in semnames}
    block = es.enter_context(nc.Block())

    XB = [4, 4, 4, 4]

    _vg = [0]
    _sg = [0]

    def vguard(eng, ins):
        _vg[0] += 1
        ins.then_inc(sem["v_g"], 1)
        eng.wait_ge(sem["v_g"], _vg[0])

    def sguard(eng, ins):
        _sg[0] += 1
        ins.then_inc(sem["s_g"], 1)
        eng.wait_ge(sem["s_g"], _sg[0])

    def xsem(t):
        return sem[f"ld_x{t % 4}"], 16 * (t // 4 + 1)

    def xsemB(t):
        return sem[f"ld_x{t % 4}"], 16 * (XB[t % 4] + t // 4 + 1)

    def xsem2(i):
        return sem[f"ld_x{i % 4}"], 16 * (2 * XB[i % 4] + i // 4 + 1)

    XLAST = [13, 14, 15, 16]

    def wsem(s_):
        if s_ >= 16:
            return sem["ld_wv"], 16 * (s_ - 16 + 1)
        return sem[f"ld_w{s_ % 2}"], 16 * (s_ // 2 + 1)

    def vcc(stage):
        return stage + 9

    def e1sem(n):
        return sem["ld_e1a" if n % 2 == 0 else "ld_e1b"], 16 * (n // 2 + 1)

    def stsem(j):
        return sem[f"st{j % 2}"], 16 * (j // 2 + 1)


    @block.sync
    def _(sync):
        sync.dma_start(out=ident[:], in_=id_hbm).then_inc(sem["ld_c"], 16)
        sync.dma_start(out=bq_t[:], in_=bq_hbm).then_inc(sem["ld_c"], 16)
        sync.dma_start(out=bv_row[:], in_=bv_hbm).then_inc(sem["ld_c"], 16)
        sync.dma_start(out=bf_row[:], in_=bf_hbm).then_inc(sem["ld_c"], 16)
        sync.dma_start(out=ones_r[:], in_=on_hbm).then_inc(sem["ld_c"], 16)
        for t in range(NT):
            if t >= 4:
                sync.wait_ge(sem["v_r0"], t - 3)
            sync.dma_start(out=x_t[t % 4][:], in_=x_hbm[t * 128:(t + 1) * 128, :]
                           ).then_inc(xsem(t)[0], 16)
            if 1 <= t <= KC:
                j = t - 1
                if j <= 1:
                    sync.wait_ge(sem["t_bc"], 1)
                else:
                    sync.wait_ge(sem["v_cast"], j - 1)
                sync.dma_start(out=wv_f32[j % 2][:], in_=wv_hbm[j]).then_inc(wsem(16 + j)[0], 16)
        for t in range(NT):
            if t < 4:
                sync.wait_ge(sem["v_r0"], XLAST[t % 4])
            else:
                sync.wait_ge(sem["v_emit"], t - 3)
            sync.dma_start(out=x_t[t % 4][:], in_=x_hbm[t * 128:(t + 1) * 128, :]
                           ).then_inc(xsemB(t)[0], 16)
        sync.wait_ge(sem["t_bc"], 1)
        sync.wait_ge(sem["t_q"], 16)
        for m in range(2):
            if m == 1:
                sync.wait_ge(sem["v_wbm"], 16)
            for c in range(4):
                sync.dma_start(out=e2_sb[:, c * 512:(c + 1) * 512],
                               in_=e2_hbm[m, c]).then_inc(sem["ld_e2"], 16)
            for chunk in range(4):
                for c in range(4):
                    n = m * 16 + chunk * 4 + c
                    if n >= 2:
                        sync.wait_ge(sem["v_wbm"], n - 1)
                    sync.dma_start(out=e1_t[n % 2][:],
                                   in_=e1_hbm[m, c, chunk * 128:(chunk + 1) * 128, :]
                                   ).then_inc(e1sem(n)[0], 16)
        for i in range(NT):
            if i <= 3:
                sync.wait_ge(sem["v_emit"], XLAST[i % 4])
            else:
                sync.wait_ge(sem["v_y"], i - 3)
            sync.dma_start(out=x_t[i % 4][:], in_=x_hbm[i * 128:(i + 1) * 128, :]
                           ).then_inc(xsem2(i)[0], 16)

    @block.tensor
    def _(tensor):
        tensor.wait_ge(sem["ld_c"], 16 * 5)
        tensor.matmul(out=psv[0][:, 0:512], lhsT=ones_r[:], rhs=bv_row[:, 0:512],
                      start=True, stop=True)
        tensor.matmul(out=psv[0][:, 512:1024], lhsT=ones_r[:], rhs=bv_row[:, 512:1024],
                      start=True, stop=True)
        tensor.matmul(out=psv[1][:, 0:512], lhsT=ones_r[:], rhs=bf_row[:, 0:512],
                      start=True, stop=True)
        tensor.matmul(out=psv[1][:, 512:1024], lhsT=ones_r[:], rhs=bf_row[:, 512:1024],
                      start=True, stop=True).then_inc(sem["t_bc"], 1)

        def v_group(tc):
            tensor.wait_ge(sem["s_scF"], 2 * tc + 2)
            if tc >= 2:
                tensor.wait_ge(sem["v_vt"], tc - 1)
            for k in range(KC):
                tensor.matmul(out=psv[tc % 2][:, 0:512],
                              lhsT=scf[:, k * S + tc * 128: k * S + (tc + 1) * 128],
                              rhs=wv_bf[:, k * D: k * D + 512],
                              start=(k == 0), stop=(k == KC - 1))
                ins = tensor.matmul(out=psv[tc % 2][:, 512:1024],
                                    lhsT=scf[:, k * S + tc * 128: k * S + (tc + 1) * 128],
                                    rhs=wv_bf[:, k * D + 512: (k + 1) * D],
                                    start=(k == 0), stop=(k == KC - 1))
            ins.then_inc(sem["t_q"], 1)

        tensor.wait_ge(sem["v_bc"], 2)
        for t in range(NT):
            tensor.wait_ge(sem["v_emit"], t + 1)
            if t >= 1:
                tensor.wait_ge(sem["s_scF"], 2 * t - 1)
            for k in range(4):
                ins = tensor.matmul(out=psq[0][:, k * 128:(k + 1) * 128],
                                    lhsT=sctok[t % 2][:, k * 128:(k + 1) * 128],
                                    rhs=ident[:], start=True, stop=True)
            ins.then_inc(sem["t_scT"], 1)
            if t >= 1:
                tensor.wait_ge(sem["s_scF"], 2 * t)
            for k in range(4, 8):
                ins = tensor.matmul(out=psq[1][:, (k - 4) * 128:(k - 3) * 128],
                                    lhsT=sctok[t % 2][:, k * 128:(k + 1) * 128],
                                    rhs=ident[:], start=True, stop=True)
            ins.then_inc(sem["t_scT"], 1)
            if t == 0:
                tensor.wait_ge(sem["v_cast"], 8)
            if t >= 1:
                v_group(t - 1)
        v_group(NT - 1)
        def qbank(oc, sl):
            if oc % 2 == 0:
                return psq[sl][:]
            return psv[sl // 2][:, (sl % 2) * 512:(sl % 2) * 512 + 512]

        tensor.wait_ge(sem["v_vt"], NT)
        for oc in range(KC):
            tensor.wait_ge(sem["v_cast"], vcc(oc))
            if oc >= 2:
                tensor.wait_ge(sem["s_qf"], 4 * oc - 4)
            for k in range(KC):
                for sl in range(4):
                    ins = tensor.matmul(out=qbank(oc, sl),
                                        lhsT=w_bf[oc % 2][:, k * 128:(k + 1) * 128],
                                        rhs=scf[:, k * S + sl * 512: k * S + (sl + 1) * 512],
                                        start=(k == 0), stop=(k == KC - 1))
                    if k == KC - 1:
                        ins.then_inc(sem["t_q"], 1)
        for k2 in range(KC):
            tensor.wait_ge(sem["v_cast"], vcc(8 + k2))
            tensor.wait_ge(sem["s_qf"], max(32, 28 + 4 * k2))
            for h in range(KC):
                for sl in range(4):
                    ins = tensor.matmul(out=qbank(k2, sl),
                                        lhsT=w_bf[k2 % 2][:, h * 128:(h + 1) * 128],
                                        rhs=qf[:, h * S + sl * 512: h * S + (sl + 1) * 512],
                                        start=(h == 0), stop=(h == KC - 1))
                    if h == KC - 1:
                        ins.then_inc(sem["t_q"], 1)

        tensor.wait_ge(sem["s_qf"], 64)
        tensor.wait_ge(sem["v_vt"], NT)
        tensor.wait_ge(sem["v_wb"], 8)

        sbank = [psA, psB, psC, psD]

        def scores_mms(j):
            for sl in range(4):
                if j >= 1:
                    tensor.wait_ge(sem["v_scev"], 4 * (j - 1) + sl + 1)
                if sl < 2 and j >= 3:
                    tensor.wait_ge(sem["s_lF"], 2 * (j - 3) + sl + 1)
                for k in range(KC):
                    ins = tensor.matmul(
                        out=sbank[sl][:],
                        lhsT=qf[:, k * S + j * 128: k * S + (j + 1) * 128],
                        rhs=tqf[:, k * S + sl * 512: k * S + (sl + 1) * 512],
                        start=(k == 0), stop=(k == KC - 1))
                ins.then_inc(sem["t_sc"], 1)

        scores_mms(0)
        for i in range(NT):
            if i < NT - 1:
                scores_mms(i + 1)
            tensor.wait_ge(sem["s_exp"], i + 1)
            if i >= 1:
                tensor.wait_ge(sem["v_ctxev"], i)
            if i >= 2:
                tensor.wait_ge(sem["v_ffnev"], i - 1)
            for f in range(4):
                dst = psE if f < 2 else psG
                half = (f % 2) * 512
                for jj in range(4):
                    j = f * 4 + jj
                    ins = tensor.matmul(out=dst[:, half + jj * 128: half + (jj + 1) * 128],
                                        lhsT=probs[i % 2][:, j * 128:(j + 1) * 128],
                                        rhs=ident[:], start=True, stop=True)
                ins.then_inc(sem["t_pT"], 1)
            if i >= 1:
                tensor.wait_ge(sem["v_ln2e"], i)
                for f in range(2):
                    if i < NT - 1:
                        tensor.wait_ge(sem["v_scev"], 4 * (i + 1) + f + 1)
                    else:
                        tensor.wait_ge(sem["s_lF"], 2 * (i - 2) + f + 1)
                    for kk in range(4):
                        k = f * 4 + kk
                        ins = tensor.matmul(out=psAB[f][:, kk * 128:(kk + 1) * 128],
                                            lhsT=ln2tok[(i - 1) % 2][:, k * 128:(k + 1) * 128],
                                            rhs=ident[:], start=True, stop=True)
                    ins.then_inc(sem["t_lT"], 1)
            tensor.wait_ge(sem["s_pT"], 4 * i + 4)
            for j in range(NT):
                if False:
                    pass
                tensor.matmul(out=psE[:, 0:512],
                              lhsT=probsT[:, j * 128:(j + 1) * 128],
                              rhs=vt[:, j * D: j * D + 512],
                              start=(j == 0), stop=(j == NT - 1))
                ins = tensor.matmul(out=psE[:, 512:1024],
                                    lhsT=probsT[:, j * 128:(j + 1) * 128],
                                    rhs=vt[:, j * D + 512: (j + 1) * D],
                                    start=(j == 0), stop=(j == NT - 1))
            ins.then_inc(sem["t_ctx"], 1)
            if i >= 1:
                tensor.wait_ge(sem["s_lF"], 2 * i)
                tensor.wait_ge(sem["v_ffnev"], i - 1)
                for k in range(KC):
                    tensor.matmul(out=psG[:, 0:512],
                                  lhsT=ln2F[(i - 1) % 2][:, k * 128:(k + 1) * 128],
                                  rhs=wbig[:, k * D: k * D + 512],
                                  start=(k == 0), stop=(k == KC - 1))
                    ins = tensor.matmul(out=psG[:, 512:1024],
                                        lhsT=ln2F[(i - 1) % 2][:, k * 128:(k + 1) * 128],
                                        rhs=wbig[:, k * D + 512: (k + 1) * D],
                                        start=(k == 0), stop=(k == KC - 1))
                ins.then_inc(sem["t_ffn"], 1)
        i = NT
        tensor.wait_ge(sem["v_ln2e"], NT)
        for f in range(2):
            tensor.wait_ge(sem["s_lF"], 2 * (NT - 2) + f + 1)
            for kk in range(4):
                k = f * 4 + kk
                ins = tensor.matmul(out=psAB[f][:, kk * 128:(kk + 1) * 128],
                                    lhsT=ln2tok[(NT - 1) % 2][:, k * 128:(k + 1) * 128],
                                    rhs=ident[:], start=True, stop=True)
            ins.then_inc(sem["t_lT"], 1)
        tensor.wait_ge(sem["s_lF"], 2 * NT)
        tensor.wait_ge(sem["v_ffnev"], NT - 1)
        for k in range(KC):
            tensor.matmul(out=psG[:, 0:512],
                          lhsT=ln2F[(NT - 1) % 2][:, k * 128:(k + 1) * 128],
                          rhs=wbig[:, k * D: k * D + 512],
                          start=(k == 0), stop=(k == KC - 1))
            ins = tensor.matmul(out=psG[:, 512:1024],
                                lhsT=ln2F[(NT - 1) % 2][:, k * 128:(k + 1) * 128],
                                rhs=wbig[:, k * D + 512: (k + 1) * D],
                                start=(k == 0), stop=(k == KC - 1))
        ins.then_inc(sem["t_ffn"], 1)

    @block.vector
    def _(vector):
        vector.memset(epsc[:], EPS)
        vector.wait_ge(sem["t_bc"], 1)
        vector.tensor_copy(out=bv_bc[:], in_=psv[0][:]).then_inc(sem["v_bc"], 1)
        vector.tensor_copy(out=bf_bc[:], in_=psv[1][:]).then_inc(sem["v_bc"], 1)
        def wv_cast(j):
            vector.wait_ge(*wsem(16 + j))
            vector.tensor_copy(out=wv_bf[:, j * D:(j + 1) * D], in_=wv_f32[j % 2][:]
                               ).then_inc(sem["v_cast"], 1)

        for t in range(NT):
            vector.wait_ge(*xsem(t))
            vector.bn_stats(out=bn12[:, 0:6], in_=x_t[t % 4][:, 0:512])
            vguard(vector, vector.bn_stats(out=bn12[:, 6:12], in_=x_t[t % 4][:, 512:1024]))
            vector.bn_aggr(out=mv_all[:, 2 * t:2 * t + 2], in_=bn12[:]
                           ).then_inc(sem["v_r0"], 1)
            if t < KC:
                wv_cast(t)
        vector.wait_ge(sem["s_inv0"], 1)
        vguard(vector, vector.tensor_tensor(out=norm_all[:], in0=sa16[:],
                                            in1=inv16[:], op=ALU.mult))
        vguard(vector, vector.tensor_scalar(out=vtmp[:], in0=norm_all[:],
                                            scalar1=NEPS, scalar2=None, op0=ALU.add))
        vguard(vector, vector.reciprocal(out=rn16[:], in_=vtmp[:]))
        vguard(vector, vector.tensor_tensor(out=k16[:], in0=inv16[:], in1=rn16[:],
                                            op=ALU.mult))
        for t in range(NT):
            vector.wait_ge(*xsemB(t))
            if t >= 1:
                vector.wait_ge(sem["t_scT"], 2 * t)
            vector.tensor_scalar(out=sctok[t % 2][:], in0=x_t[t % 4][:],
                                 scalar1=mv_all[:, 2 * t:2 * t + 1], scalar2=k16[:, t:t + 1],
                                 op0=ALU.subtract, op1=ALU.mult
                                 ).then_inc(sem["v_emit"], 1)
            if t >= 2:
                vector.wait_ge(sem["t_q"], t - 1)
                vector.tensor_tensor(out=vt[:, (t - 2) * D:(t - 1) * D], in0=psv[t % 2][:],
                                     in1=bv_bc[:], op=ALU.add).then_inc(sem["v_vt"], 1)
        for tc in range(NT - 2, NT):
            vector.wait_ge(sem["t_q"], tc + 1)
            vector.tensor_tensor(out=vt[:, tc * D:(tc + 1) * D], in0=psv[tc % 2][:],
                                 in1=bv_bc[:], op=ALU.add).then_inc(sem["v_vt"], 1)
        def wbuild_unit(u):
            m, chunk = u // 4, u % 4
            for c in range(4):
                n = m * 16 + chunk * 4 + c
                vector.wait_ge(*e1sem(n))
                vector.wait_ge(sem["ld_e2"], 64 * (m + 1))
                if c == 0:
                    vector.tensor_tensor(out=wacc[:], in0=e1_t[n % 2][:],
                                         in1=e2_sb[:, c * 512:(c + 1) * 512],
                                         op=ALU.mult).then_inc(sem["v_wbm"], 1)
                else:
                    vector.tensor_tensor(out=wtmp[:], in0=e1_t[n % 2][:],
                                         in1=e2_sb[:, c * 512:(c + 1) * 512],
                                         op=ALU.mult).then_inc(sem["v_wbm"], 1)
                    vector.tensor_tensor(out=wacc[:], in0=wacc[:], in1=wtmp[:],
                                         op=ALU.add)
            if m == 0:
                vector.tensor_scalar(out=wbig[:, chunk * D: chunk * D + 512],
                                     in0=wacc[:], scalar1=1.0, scalar2=None,
                                     op0=ALU.mult)
                vector.tensor_scalar(out=wbig[:, (chunk + 4) * D + 512: (chunk + 5) * D],
                                     in0=wacc[:], scalar1=1.0, scalar2=None,
                                     op0=ALU.mult).then_inc(sem["v_wb"], 1)
            else:
                vector.tensor_scalar(out=wbig[:, chunk * D + 512: (chunk + 1) * D],
                                     in0=wacc[:], scalar1=1.0, scalar2=None,
                                     op0=ALU.mult)
                vector.tensor_scalar(out=wbig[:, (chunk + 4) * D: (chunk + 4) * D + 512],
                                     in0=wacc[:], scalar1=-1.0, scalar2=None,
                                     op0=ALU.mult).then_inc(sem["v_wb"], 1)

        for s_ in range(16):
            vector.wait_ge(*wsem(s_))
            if s_ >= 2:
                vector.wait_ge(sem["t_q"], 16 + 4 * (s_ - 1))
            vector.tensor_copy(out=w_bf[s_ % 2][:], in_=w_f32[0][:]
                               ).then_inc(sem["v_cast"], 1)
            if 2 <= s_ <= 9:
                wbuild_unit(s_ - 2)
        vector.wait_ge(sem["v_scev"], 4)
        vector.reduce_max(out=negm_all[:, 0:1], in_=sc_sb[0][:],
                          axis=AX.X, negate=True).then_inc(sem["v_max"], 1)
        for i in range(NT):
            vector.wait_ge(sem["s_exp"], i + 1)
            vguard(vector, vector.reciprocal(out=rs_s[:], in_=sume_all[:, i:i + 1]))
            vguard(vector, vector.tensor_tensor(out=csc[:], in0=rs_s[:],
                                                in1=norm_all[:, i:i + 1], op=ALU.mult))
            if i < NT - 1:
                vector.wait_ge(sem["v_scev"], 4 * (i + 1) + 4)
                vector.reduce_max(out=negm_all[:, i + 1:i + 2], in_=sc_sb[(i + 1) % 2][:],
                                  axis=AX.X, negate=True).then_inc(sem["v_max"], 1)
            vector.wait_ge(sem["t_ctx"], i + 1)
            vector.tensor_scalar(out=attn[:], in0=psE[:], scalar1=csc[:], scalar2=None,
                                 op0=ALU.mult).then_inc(sem["v_ctxev"], 1)
            if i >= 1:
                vector.wait_ge(sem["t_ffn"], i)
                vector.tensor_tensor(out=ffn_f[:], in0=psG[:], in1=bf_bc[:],
                                     op=ALU.add).then_inc(sem["v_ffnev"], 1)
            vector.bn_stats(out=bn12[:, 0:6], in_=attn[:, 0:512])
            vguard(vector, vector.bn_stats(out=bn12[:, 6:12], in_=attn[:, 512:1024]))
            vector.bn_aggr(out=mvA[:], in_=bn12[:]).then_inc(sem["v_rA"], 1)
            vector.wait_ge(sem["s_invA"], i + 1)
            vector.tensor_scalar(out=ytmp[:], in0=attn[:], scalar1=mvA[:, 0:1],
                                 scalar2=invA[:], op0=ALU.subtract, op1=ALU.mult)
            vector.wait_ge(*xsem2(i))
            vector.tensor_tensor(out=y_buf[i % 3][:], in0=ytmp[:], in1=x_t[i % 4][:],
                                 op=ALU.add).then_inc(sem["v_y"], 1)
            vector.bn_stats(out=bn12[:, 0:6], in_=y_buf[i % 3][:, 0:512])
            vguard(vector, vector.bn_stats(out=bn12[:, 6:12], in_=y_buf[i % 3][:, 512:1024]))
            vector.bn_aggr(out=mvL[:], in_=bn12[:]).then_inc(sem["v_rL"], 1)
            vector.wait_ge(sem["s_invL"], i + 1)
            if i >= 2:
                vector.wait_ge(sem["t_lT"], 2 * (i - 2) + 2)
            vector.tensor_scalar(out=ln2tok[i % 2][:], in0=y_buf[i % 3][:],
                                 scalar1=mvL[:, 0:1], scalar2=invL[:],
                                 op0=ALU.subtract, op1=ALU.mult
                                 ).then_inc(sem["v_ln2e"], 1)
            if i >= 1:
                vector.bn_stats(out=bn12[:, 0:6], in_=ffn_f[:, 0:512])
                vguard(vector, vector.bn_stats(out=bn12[:, 6:12], in_=ffn_f[:, 512:1024]))
                vector.bn_aggr(out=mvF[:], in_=bn12[:]).then_inc(sem["v_rF"], 1)
                vector.wait_ge(sem["s_invF"], i)
                vector.tensor_scalar(out=ytmp[:], in0=ffn_f[:], scalar1=mvF[:, 0:1],
                                     scalar2=invF[:], op0=ALU.subtract, op1=ALU.mult)
                if i - 1 >= 2:
                    vector.wait_ge(*stsem(i - 3))
                vector.tensor_tensor(out=out_b[(i - 1) % 2][:], in0=ytmp[:],
                                     in1=y_buf[(i - 1) % 3][:], op=ALU.add
                                     ).then_inc(sem["v_out"], 1)
        vector.wait_ge(sem["t_ffn"], NT)
        vector.tensor_tensor(out=ffn_f[:], in0=psG[:], in1=bf_bc[:],
                             op=ALU.add).then_inc(sem["v_ffnev"], 1)
        vector.bn_stats(out=bn12[:, 0:6], in_=ffn_f[:, 0:512])
        vguard(vector, vector.bn_stats(out=bn12[:, 6:12], in_=ffn_f[:, 512:1024]))
        vector.bn_aggr(out=mvF[:], in_=bn12[:]).then_inc(sem["v_rF"], 1)
        vector.wait_ge(sem["s_invF"], NT)
        vector.tensor_scalar(out=ytmp[:], in0=ffn_f[:], scalar1=mvF[:, 0:1],
                             scalar2=invF[:], op0=ALU.subtract, op1=ALU.mult)
        vector.wait_ge(*stsem(NT - 3))
        vector.tensor_tensor(out=out_b[(NT - 1) % 2][:], in0=ytmp[:],
                             in1=y_buf[(NT - 1) % 3][:], op=ALU.add
                             ).then_inc(sem["v_out"], 1)

    @block.scalar
    def _(scalar):
        scalar.wait_ge(sem["t_bc"], 1)
        scalar.dma_start(out=w_f32[0][:], in_=wq_hbm[0]).then_inc(sem["ld_w0"], 16)
        scalar.wait_ge(sem["ld_c"], 16 * 5)
        scalar.wait_ge(sem["v_r0"], NT)
        var16 = mv_all[:].rearrange("p (t two) -> p two t", two=2)[:, 1, :]
        sguard(scalar, scalar.activation(out=lns16[:], in_=var16, func=AF.Ln, bias=epsc[:]))
        scalar.activation(out=inv16[:], in_=lns16[:], func=AF.Exp, scale=-0.5)
        sguard(scalar, scalar.activation(out=lns16[:], in_=var16, func=AF.Ln, scale=1024.0))
        scalar.activation(out=sa16[:], in_=lns16[:], func=AF.Exp, scale=0.5
                          ).then_inc(sem["s_inv0"], 1)
        for t in range(NT):
            scf4 = scf[:].rearrange("p (k t j) -> p k t j", k=KC, t=NT, j=128)
            scalar.wait_ge(sem["t_scT"], 2 * t + 1)
            scalar.copy(out=scf4[:, 0:4, t, :],
                        in_=psq[0][:].rearrange("p (k j) -> p k j", k=4)
                        ).then_inc(sem["s_scF"], 1)
            scalar.wait_ge(sem["t_scT"], 2 * t + 2)
            scalar.copy(out=scf4[:, 4:8, t, :],
                        in_=psq[1][:].rearrange("p (k j) -> p k j", k=4)
                        ).then_inc(sem["s_scF"], 1)
        scalar.wait_ge(sem["v_cast"], vcc(0))
        scalar.dma_start(out=w_f32[0][:], in_=wq_hbm[1]).then_inc(sem["ld_w1"], 16)
        for oc in range(KC):
            if oc + 2 < KC:
                scalar.wait_ge(sem["v_cast"], vcc(oc + 1))
                scalar.dma_start(out=w_f32[0][:], in_=wq_hbm[oc + 2]
                                 ).then_inc(wsem(oc + 2)[0], 16)
            elif oc + 2 < KC + 2:
                scalar.wait_ge(sem["v_cast"], vcc(oc + 1))
                scalar.dma_start(out=w_f32[0][:], in_=am_hbm[oc + 2 - KC]
                                 ).then_inc(wsem(oc + 2)[0], 16)
            for sl in range(4):
                scalar.wait_ge(sem["t_q"], 16 + 4 * oc + sl + 1)
                src_b = (psq[sl][:] if oc % 2 == 0
                         else psv[sl // 2][:, (sl % 2) * 512:(sl % 2) * 512 + 512])
                scalar.add(out=qf[:, oc * S + sl * 512: oc * S + (sl + 1) * 512],
                           in_=src_b, add=bq_t[:, oc:oc + 1]).then_inc(sem["s_qf"], 1)
        for k2 in range(KC):
            if k2 + 2 < KC:
                scalar.wait_ge(sem["v_cast"], vcc(9 + k2))
                scalar.dma_start(out=w_f32[0][:], in_=am_hbm[k2 + 2]
                                 ).then_inc(wsem(k2 + 10)[0], 16)
            for sl in range(4):
                scalar.wait_ge(sem["t_q"], 48 + 4 * k2 + sl + 1)
                src_b = (psq[sl][:] if k2 % 2 == 0
                         else psv[sl // 2][:, (sl % 2) * 512:(sl % 2) * 512 + 512])
                scalar.copy(out=tqf[:, k2 * S + sl * 512: k2 * S + (sl + 1) * 512],
                            in_=src_b).then_inc(sem["s_qf"], 1)
        sbank_s = [psA, psB, psC, psD]

        def sc_evac(j):
            for sl in range(4):
                g = 4 * j + sl
                scalar.wait_ge(sem["t_sc"], g + 1)
                scalar.mul(out=sc_sb[j % 2][:, sl * 512:(sl + 1) * 512],
                           in_=sbank_s[sl][:], mul=ISQ).then_inc(sem["v_scev"], 1)

        def s_exp(j):
            scalar.wait_ge(sem["v_max"], j + 1)
            if j >= 2:
                scalar.wait_ge(sem["t_pT"], 4 * (j - 2) + 4)
            scalar.activation(out=probs[j % 2][:], in_=sc_sb[j % 2][:], func=AF.Exp,
                              bias=negm_all[:, j:j + 1],
                              accum_out=sume_all[:, j:j + 1]).then_inc(sem["s_exp"], 1)

        def sc_evac_part(j, sls):
            for sl in sls:
                g = 4 * j + sl
                scalar.wait_ge(sem["t_sc"], g + 1)
                scalar.mul(out=sc_sb[j % 2][:, sl * 512:(sl + 1) * 512],
                           in_=sbank_s[sl][:], mul=ISQ).then_inc(sem["v_scev"], 1)

        def s_invL(j):
            scalar.wait_ge(sem["v_rL"], j + 1)
            sguard(scalar, scalar.activation(out=lns[:], in_=mvL[:, 1:2], func=AF.Ln, bias=epsc[:]))
            scalar.activation(out=invL[:], in_=lns[:], func=AF.Exp, scale=-0.5
                              ).then_inc(sem["s_invL"], 1)

        def s_invF(j):
            scalar.wait_ge(sem["v_rF"], j + 1)
            sguard(scalar, scalar.activation(out=lns[:], in_=mvF[:, 1:2], func=AF.Ln, bias=epsc[:]))
            scalar.activation(out=invF[:], in_=lns[:], func=AF.Exp, scale=-0.5
                              ).then_inc(sem["s_invF"], 1)

        def s_store(j):
            scalar.wait_ge(sem["v_out"], j + 1)
            if j >= 2:
                scalar.wait_ge(*stsem(j - 2))
            scalar.dma_start(out=y_hbm[j * 128: (j + 1) * 128, :],
                             in_=out_b[j % 2][:]).then_inc(stsem(j)[0], 16)

        sc_evac(0)
        s_exp(0)
        for i in range(NT):
            if i < NT - 1:
                scalar.wait_ge(sem["v_max"], i + 1)
                sc_evac_part(i + 1, (0, 1))
            if i >= 1:
                s_invL(i - 1)
            if i < NT - 1:
                sc_evac_part(i + 1, (2, 3))
            if i >= 1:
                scalar.wait_ge(sem["t_ctx"], i)
            for f in range(4):
                scalar.wait_ge(sem["t_pT"], 4 * i + f + 1)
                srcp = psE if f < 2 else psG
                scalar.copy(out=probsT[:, f * 512:(f + 1) * 512],
                            in_=srcp[:, (f % 2) * 512:(f % 2) * 512 + 512]
                            ).then_inc(sem["s_pT"], 1)
            if i >= 1:
                if i >= 2:
                    scalar.wait_ge(sem["t_ffn"], i - 1)
                for f in range(2):
                    scalar.wait_ge(sem["t_lT"], 2 * (i - 1) + f + 1)
                    scalar.copy(out=ln2F[(i - 1) % 2][:, f * 512:(f + 1) * 512],
                                in_=psAB[f][:]).then_inc(sem["s_lF"], 1)
            if i >= 2:
                s_invF(i - 2)
            if i < NT - 1:
                s_exp(i + 1)
            if i >= 2:
                s_store(i - 2)
            scalar.wait_ge(sem["v_rA"], i + 1)
            sguard(scalar, scalar.activation(out=lns[:], in_=mvA[:, 1:2], func=AF.Ln, bias=epsc[:]))
            scalar.activation(out=invA[:], in_=lns[:], func=AF.Exp, scale=-0.5
                              ).then_inc(sem["s_invA"], 1)
        s_invL(NT - 1)
        for f in range(2):
            scalar.wait_ge(sem["t_lT"], 2 * (NT - 1) + f + 1)
            scalar.copy(out=ln2F[(NT - 1) % 2][:, f * 512:(f + 1) * 512],
                        in_=psAB[f][:]).then_inc(sem["s_lF"], 1)
        s_invF(NT - 2)
        s_store(NT - 2)
        s_invF(NT - 1)
        s_store(NT - 1)
        scalar.wait_ge(sem["st0"], 16 * 8)
        scalar.wait_ge(sem["st1"], 16 * 8)

    if debug:
        @block.gpsimd
        def _(gpsimd):
            gpsimd.wait_ge(sem["st0"], 16 * 8)
            gpsimd.wait_ge(sem["st1"], 16 * 8)
            gpsimd.dma_start(out=dbg["d_qf"], in_=qf[:]).then_inc(sem["dbg"], 16)
            gpsimd.dma_start(out=dbg["d_tqf"], in_=tqf[:]).then_inc(sem["dbg"], 16)
            gpsimd.dma_start(out=dbg["d_vt"], in_=vt[:]).then_inc(sem["dbg"], 16)
            gpsimd.dma_start(out=dbg["d_wbig"], in_=wbig[:]).then_inc(sem["dbg"], 16)
            gpsimd.dma_start(out=dbg["d_norm"], in_=norm_all[:]).then_inc(sem["dbg"], 16)
            gpsimd.dma_start(out=dbg["d_sums"], in_=sume_all[:]).then_inc(sem["dbg"], 16)
            gpsimd.wait_ge(sem["dbg"], 16 * 6)

    es.close()
    return nc


def _tile8(w):
    return np.ascontiguousarray(
        np.transpose(w.reshape(8, 128, 8, 128), (2, 1, 0, 3)).reshape(8, 128, 1024))


def stage_inputs(L0_attn_hidden_states, L0_attn_Wq, L0_attn_bq, L0_attn_Wv, L0_attn_bv,
                 L0_attn_anomaly_matrix, L0_ffn_G1_R, L0_ffn_G2_R, L0_ffn_G1_I,
                 L0_ffn_G2_I, L0_ffn_bias_real, L0_ffn_bias_imag):
    f32 = np.float32
    x = np.asarray(L0_attn_hidden_states, dtype=f32)
    Wq = np.asarray(L0_attn_Wq, dtype=f32)
    Wv = np.asarray(L0_attn_Wv, dtype=f32)
    A = np.asarray(L0_attn_anomaly_matrix, dtype=f32)
    bq = np.asarray(L0_attn_bq, dtype=f32)
    bv = np.asarray(L0_attn_bv, dtype=f32)
    G1R = np.asarray(L0_ffn_G1_R, dtype=f32)
    G2R = np.asarray(L0_ffn_G2_R, dtype=f32)
    G1I = np.asarray(L0_ffn_G1_I, dtype=f32)
    G2I = np.asarray(L0_ffn_G2_I, dtype=f32)
    br = np.asarray(L0_ffn_bias_real, dtype=f32)
    bi = np.asarray(L0_ffn_bias_imag, dtype=f32)

    common = dict(
        wq_t=_tile8(np.ascontiguousarray(Wq.T)),
        amat_t=_tile8(A),
        wv_t=np.ascontiguousarray(Wv.T).reshape(8, 128, 1024),
        bq_t=np.ascontiguousarray(bq.reshape(8, 128).T),
        bv_row=bv.reshape(1, 1024),
        brbi_row=np.concatenate([br, bi]).reshape(1, 1024),
        ones_row=np.ones((1, 128), dtype=f32),
        ident=np.eye(128, dtype=ml_dtypes.bfloat16),
        e1=np.stack([
            np.stack([np.repeat(np.repeat(np.ascontiguousarray(G1[0, :, c, :].T), 16, 0),
                                16, 1) for c in range(4)])
            for G1 in (G1R, G1I)]).astype(f32),
        e2=np.stack([
            np.stack([np.tile(np.ascontiguousarray(G2[c, :, 0, :].T), (8, 32))
                      for c in range(4)])
            for G2 in (G2R, G2I)]).astype(f32),
    )
    in_maps = [dict(common, x=np.ascontiguousarray(x[b])) for b in range(8)]
    return in_maps


def kernel(**inputs):
    in_maps = stage_inputs(**inputs)
    nc = build_nc()
    res = run_bass_kernel_spmd(nc, in_maps, core_ids=list(range(8)))
    return np.stack([res.results[c]["y_out"] for c in range(8)], axis=0)


# version 15
# speedup vs baseline: 1.0202x; 1.0202x over previous
import numpy as np
import ml_dtypes
from contextlib import ExitStack
import concourse.bass as bass
import concourse.mybir as mybir
from concourse.bass_utils import run_bass_kernel_spmd

F32 = mybir.dt.float32
BF16 = mybir.dt.bfloat16
AF = mybir.ActivationFunctionType
ALU = mybir.AluOpType
AX = mybir.AxisListType

D = 1024
S = 2048
NT = 16
KC = 8
EPS = 1e-5
NEPS = 1e-9
ISQ = 1.0 / 32.0


def build_nc(debug=False):
    nc = bass.Bass("TRN2", target_bir_lowering=False, detect_race_conditions=False)

    x_hbm = nc.dram_tensor("x", [S, D], F32, kind="ExternalInput").ap()
    wq_hbm = nc.dram_tensor("wq_t", [KC, 128, D], F32, kind="ExternalInput").ap()
    am_hbm = nc.dram_tensor("amat_t", [KC, 128, D], F32, kind="ExternalInput").ap()
    wv_hbm = nc.dram_tensor("wv_t", [KC, 128, D], F32, kind="ExternalInput").ap()
    bq_hbm = nc.dram_tensor("bq_t", [128, KC], F32, kind="ExternalInput").ap()
    bv_hbm = nc.dram_tensor("bv_row", [1, D], F32, kind="ExternalInput").ap()
    bf_hbm = nc.dram_tensor("brbi_row", [1, D], F32, kind="ExternalInput").ap()
    on_hbm = nc.dram_tensor("ones_row", [1, 128], F32, kind="ExternalInput").ap()
    id_hbm = nc.dram_tensor("ident", [128, 128], BF16, kind="ExternalInput").ap()
    e1_hbm = nc.dram_tensor("e1", [2, 4, 512, 512], F32, kind="ExternalInput").ap()
    e2_hbm = nc.dram_tensor("e2", [2, 4, 128, 512], F32, kind="ExternalInput").ap()

    y_hbm = nc.dram_tensor("y_out", [S, D], F32, kind="ExternalOutput").ap()
    if debug:
        dbg = {
            "d_qf": nc.dram_tensor("d_qf", [128, KC * S], BF16, kind="ExternalOutput").ap(),
            "d_tqf": nc.dram_tensor("d_tqf", [128, KC * S], BF16, kind="ExternalOutput").ap(),
            "d_vt": nc.dram_tensor("d_vt", [128, NT * D], BF16, kind="ExternalOutput").ap(),
            "d_wbig": nc.dram_tensor("d_wbig", [128, KC * D], BF16, kind="ExternalOutput").ap(),
            "d_norm": nc.dram_tensor("d_norm", [128, NT], F32, kind="ExternalOutput").ap(),
            "d_sums": nc.dram_tensor("d_sums", [128, NT], F32, kind="ExternalOutput").ap(),
        }

    es = ExitStack()

    qf = es.enter_context(nc.sbuf_tensor("qf", [128, KC * S], BF16))
    tqf = es.enter_context(nc.sbuf_tensor("tqf", [128, KC * S], BF16))
    vt = es.enter_context(nc.sbuf_tensor("vt", [128, NT * D], BF16))
    wbig = es.enter_context(nc.sbuf_tensor("wbig", [128, KC * D], BF16))
    x_t = [es.enter_context(nc.sbuf_tensor(f"x_t{i}", [128, D], F32)) for i in range(4)]
    bq_t = es.enter_context(nc.sbuf_tensor("sb_bq", [128, KC], F32))
    ident = es.enter_context(nc.sbuf_tensor("sb_id", [128, 128], BF16))
    bv_bc = es.enter_context(nc.sbuf_tensor("bv_bc", [128, D], F32))
    bf_bc = es.enter_context(nc.sbuf_tensor("bf_bc", [128, D], F32))
    norm_all = es.enter_context(nc.sbuf_tensor("norm_all", [128, NT], F32))
    negm_all = es.enter_context(nc.sbuf_tensor("negm_all", [128, NT], F32))
    sume_all = es.enter_context(nc.sbuf_tensor("sume_all", [128, NT], F32))
    bn12 = es.enter_context(nc.sbuf_tensor("bn12", [128, 12], F32))
    mv_all = es.enter_context(nc.sbuf_tensor("mv_all", [128, 2 * NT], F32))
    inv16 = es.enter_context(nc.sbuf_tensor("inv16", [128, NT], F32))
    sa16 = es.enter_context(nc.sbuf_tensor("sa16", [128, NT], F32))
    k16 = es.enter_context(nc.sbuf_tensor("k16", [128, NT], F32))
    vtmp = es.enter_context(nc.sbuf_tensor("vtmp", [128, NT], F32))
    epsc = es.enter_context(nc.sbuf_tensor("epsc", [128, 1], F32))
    rn16 = es.enter_context(nc.sbuf_tensor("rn16", [128, NT], F32))
    lns16 = es.enter_context(nc.sbuf_tensor("lns16", [128, NT], F32))
    lns = es.enter_context(nc.sbuf_tensor("lns", [128, 1], F32))

    es1 = ExitStack()
    scf = es1.enter_context(nc.sbuf_tensor("scf", [128, KC * S], BF16))
    sctok = [es1.enter_context(nc.sbuf_tensor("sctok0", [128, D], BF16))] * 2
    es1r = ExitStack()
    bv_row = es1r.enter_context(nc.sbuf_tensor("sb_bvr", [1, D], F32))
    bf_row = es1r.enter_context(nc.sbuf_tensor("bf_row", [1, D], F32))
    ones_r = es1r.enter_context(nc.sbuf_tensor("ones_r", [1, 128], F32))
    es1r.close()
    wv_f32 = [es1.enter_context(nc.sbuf_tensor(f"wv_f32_{i}", [128, D], F32)) for i in range(2)]
    es1a = ExitStack()
    e1_t = [es1a.enter_context(nc.sbuf_tensor(f"e1_t{i}", [128, 512], F32)) for i in range(2)]
    e2_sb = es1a.enter_context(nc.sbuf_tensor("e2_sb", [128, 2048], F32))
    wacc = es1a.enter_context(nc.sbuf_tensor("wacc", [128, 512], F32))
    wtmp = es1a.enter_context(nc.sbuf_tensor("wtmp", [128, 512], F32))
    w_f32 = [es1a.enter_context(nc.sbuf_tensor("w_f32_0", [128, D], F32))] * 2
    w_bf = [es1a.enter_context(nc.sbuf_tensor(f"w_bf_{i}", [128, D], BF16)) for i in range(2)]
    es1a.close()
    es1b = ExitStack()
    wv_bf = es1b.enter_context(nc.sbuf_tensor("wv_bf", [128, KC * D], BF16))
    es1b.close()
    es1.close()

    es2 = ExitStack()
    sc_sb = [es2.enter_context(nc.sbuf_tensor(f"sc_sb{i}", [128, S], F32)) for i in range(2)]
    probs = [es2.enter_context(nc.sbuf_tensor(f"probs{i}", [128, S], BF16)) for i in range(2)]
    probsT = es2.enter_context(nc.sbuf_tensor("probsT", [128, S], BF16))
    attn = es2.enter_context(nc.sbuf_tensor("attn", [128, D], F32))
    y_buf = [es2.enter_context(nc.sbuf_tensor(f"y_buf{i}", [128, D], F32)) for i in range(3)]
    ln2tok = [es2.enter_context(nc.sbuf_tensor(f"ln2tok{i}", [128, D], BF16)) for i in range(2)]
    ln2F = [es2.enter_context(nc.sbuf_tensor(f"ln2F{i}", [128, D], BF16)) for i in range(2)]
    ffn_f = es2.enter_context(nc.sbuf_tensor("ffn_f", [128, D], F32))
    out_b = [es2.enter_context(nc.sbuf_tensor(f"out_b{i}", [128, D], F32)) for i in range(2)]
    mvA = es2.enter_context(nc.sbuf_tensor("mvA", [128, 2], F32))
    mvL = es2.enter_context(nc.sbuf_tensor("mvL", [128, 2], F32))
    mvF = es2.enter_context(nc.sbuf_tensor("mvF", [128, 2], F32))
    invA = es2.enter_context(nc.sbuf_tensor("invA", [128, 1], F32))
    invL = es2.enter_context(nc.sbuf_tensor("invL", [128, 1], F32))
    invF = es2.enter_context(nc.sbuf_tensor("invF", [128, 1], F32))
    rs_s = es2.enter_context(nc.sbuf_tensor("rs_s", [128, 1], F32))
    csc = es2.enter_context(nc.sbuf_tensor("csc", [128, 1], F32))
    ytmp = es2.enter_context(nc.sbuf_tensor("ytmp", [128, D], F32))
    es2.close()

    esp1 = ExitStack()
    psq = [esp1.enter_context(nc.psum_tensor(f"psq{i}", [128, 512], F32)) for i in range(4)]
    psv = [esp1.enter_context(nc.psum_tensor(f"psv{i}", [128, 1024], F32)) for i in range(2)]
    esp1.close()
    esp2 = ExitStack()
    psA = esp2.enter_context(nc.psum_tensor("psA", [128, 512], F32))
    psB = esp2.enter_context(nc.psum_tensor("psB", [128, 512], F32))
    psC = esp2.enter_context(nc.psum_tensor("psC", [128, 512], F32))
    psD = esp2.enter_context(nc.psum_tensor("psD", [128, 512], F32))
    psE = esp2.enter_context(nc.psum_tensor("psE", [128, 1024], F32))
    psG = esp2.enter_context(nc.psum_tensor("psG", [128, 1024], F32))
    esp2.close()
    psAB = [psA, psB]
    psCD = [psC, psD]

    semnames = ("ld_c ld_x0 ld_x1 ld_x2 ld_x3 ld_e1a ld_e1b ld_e2 ld_w0 ld_w1 ld_wv st0 st1 t_bc v_bc v_r0 s_inv0 v_stat v_emit t_scT s_scF "
                "v_cast t_q s_qf v_vt v_wbm v_wb dbg v_g s_g t_sc v_scev v_max s_exp t_pT s_pT "
                "t_lT s_lF v_ln2e t_ctx v_ctxev t_ffn v_ffnev v_y v_out v_rA s_invA v_rL "
                "s_invL v_rF s_invF").split()
    sem = {n: es.enter_context(nc.semaphore(n)) for n in semnames}
    block = es.enter_context(nc.Block())

    XB = [4, 4, 4, 4]

    _vg = [0]
    _sg = [0]

    def vguard(eng, ins):
        _vg[0] += 1
        ins.then_inc(sem["v_g"], 1)
        eng.wait_ge(sem["v_g"], _vg[0])

    def sguard(eng, ins):
        _sg[0] += 1
        ins.then_inc(sem["s_g"], 1)
        eng.wait_ge(sem["s_g"], _sg[0])

    def xsem(t):
        return sem[f"ld_x{t % 4}"], 16 * (t // 4 + 1)

    def xsemB(t):
        return sem[f"ld_x{t % 4}"], 16 * (XB[t % 4] + t // 4 + 1)

    def xsem2(i):
        return sem[f"ld_x{i % 4}"], 16 * (2 * XB[i % 4] + i // 4 + 1)

    XLAST = [13, 14, 15, 16]

    def wsem(s_):
        if s_ >= 16:
            return sem["ld_wv"], 16 * (s_ - 16 + 1)
        return sem[f"ld_w{s_ % 2}"], 16 * (s_ // 2 + 1)

    def vcc(stage):
        return stage + 9

    def e1sem(n):
        return sem["ld_e1a" if n % 2 == 0 else "ld_e1b"], 16 * (n // 2 + 1)

    def stsem(j):
        return sem[f"st{j % 2}"], 16 * (j // 2 + 1)


    @block.sync
    def _(sync):
        sync.dma_start(out=ident[:], in_=id_hbm).then_inc(sem["ld_c"], 16)
        sync.dma_start(out=bq_t[:], in_=bq_hbm).then_inc(sem["ld_c"], 16)
        sync.dma_start(out=bv_row[:], in_=bv_hbm).then_inc(sem["ld_c"], 16)
        sync.dma_start(out=bf_row[:], in_=bf_hbm).then_inc(sem["ld_c"], 16)
        sync.dma_start(out=ones_r[:], in_=on_hbm).then_inc(sem["ld_c"], 16)
        for t in range(NT):
            if t % 2 == 0:
                if t >= 4:
                    sync.wait_ge(sem["v_r0"], t - 3)
                sync.dma_start(out=x_t[t % 4][:], in_=x_hbm[t * 128:(t + 1) * 128, :]
                               ).then_inc(xsem(t)[0], 16)
            if 1 <= t <= KC:
                j = t - 1
                if j <= 1:
                    sync.wait_ge(sem["t_bc"], 1)
                else:
                    sync.wait_ge(sem["v_cast"], j - 1)
                sync.dma_start(out=wv_f32[j % 2][:], in_=wv_hbm[j]).then_inc(wsem(16 + j)[0], 16)
        for t in range(NT):
            if t < 4:
                sync.wait_ge(sem["v_r0"], XLAST[t % 4])
            else:
                sync.wait_ge(sem["v_emit"], t - 3)
            sync.dma_start(out=x_t[t % 4][:], in_=x_hbm[t * 128:(t + 1) * 128, :]
                           ).then_inc(xsemB(t)[0], 16)
        sync.wait_ge(sem["t_bc"], 1)
        sync.wait_ge(sem["t_q"], 16)
        for m in range(2):
            if m == 1:
                sync.wait_ge(sem["v_wbm"], 16)
            for c in range(4):
                sync.dma_start(out=e2_sb[:, c * 512:(c + 1) * 512],
                               in_=e2_hbm[m, c]).then_inc(sem["ld_e2"], 16)
            for chunk in range(4):
                for c in range(4):
                    n = m * 16 + chunk * 4 + c
                    if n >= 2:
                        sync.wait_ge(sem["v_wbm"], n - 1)
                    sync.dma_start(out=e1_t[n % 2][:],
                                   in_=e1_hbm[m, c, chunk * 128:(chunk + 1) * 128, :]
                                   ).then_inc(e1sem(n)[0], 16)
        for i in range(NT):
            if i <= 3:
                sync.wait_ge(sem["v_emit"], XLAST[i % 4])
            else:
                sync.wait_ge(sem["v_y"], i - 3)
            sync.dma_start(out=x_t[i % 4][:], in_=x_hbm[i * 128:(i + 1) * 128, :]
                           ).then_inc(xsem2(i)[0], 16)

    @block.tensor
    def _(tensor):
        tensor.wait_ge(sem["ld_c"], 16 * 5)
        tensor.matmul(out=psv[0][:, 0:512], lhsT=ones_r[:], rhs=bv_row[:, 0:512],
                      start=True, stop=True)
        tensor.matmul(out=psv[0][:, 512:1024], lhsT=ones_r[:], rhs=bv_row[:, 512:1024],
                      start=True, stop=True)
        tensor.matmul(out=psv[1][:, 0:512], lhsT=ones_r[:], rhs=bf_row[:, 0:512],
                      start=True, stop=True)
        tensor.matmul(out=psv[1][:, 512:1024], lhsT=ones_r[:], rhs=bf_row[:, 512:1024],
                      start=True, stop=True).then_inc(sem["t_bc"], 1)

        def v_group(tc):
            tensor.wait_ge(sem["s_scF"], 2 * tc + 2)
            if tc >= 2:
                tensor.wait_ge(sem["v_vt"], tc - 1)
            for k in range(KC):
                tensor.matmul(out=psv[tc % 2][:, 0:512],
                              lhsT=scf[:, k * S + tc * 128: k * S + (tc + 1) * 128],
                              rhs=wv_bf[:, k * D: k * D + 512],
                              start=(k == 0), stop=(k == KC - 1))
                ins = tensor.matmul(out=psv[tc % 2][:, 512:1024],
                                    lhsT=scf[:, k * S + tc * 128: k * S + (tc + 1) * 128],
                                    rhs=wv_bf[:, k * D + 512: (k + 1) * D],
                                    start=(k == 0), stop=(k == KC - 1))
            ins.then_inc(sem["t_q"], 1)

        tensor.wait_ge(sem["v_bc"], 2)
        for t in range(NT):
            tensor.wait_ge(sem["v_emit"], t + 1)
            if t >= 1:
                tensor.wait_ge(sem["s_scF"], 2 * t - 1)
            for k in range(4):
                ins = tensor.matmul(out=psq[0][:, k * 128:(k + 1) * 128],
                                    lhsT=sctok[t % 2][:, k * 128:(k + 1) * 128],
                                    rhs=ident[:], start=True, stop=True)
            ins.then_inc(sem["t_scT"], 1)
            if t >= 1:
                tensor.wait_ge(sem["s_scF"], 2 * t)
            for k in range(4, 8):
                ins = tensor.matmul(out=psq[1][:, (k - 4) * 128:(k - 3) * 128],
                                    lhsT=sctok[t % 2][:, k * 128:(k + 1) * 128],
                                    rhs=ident[:], start=True, stop=True)
            ins.then_inc(sem["t_scT"], 1)
            if t == 0:
                tensor.wait_ge(sem["v_cast"], 8)
            if t >= 1:
                v_group(t - 1)
        v_group(NT - 1)
        def qbank(oc, sl):
            if oc % 2 == 0:
                return psq[sl][:]
            return psv[sl // 2][:, (sl % 2) * 512:(sl % 2) * 512 + 512]

        tensor.wait_ge(sem["v_vt"], NT)
        for oc in range(KC):
            tensor.wait_ge(sem["v_cast"], vcc(oc))
            if oc >= 2:
                tensor.wait_ge(sem["s_qf"], 4 * oc - 4)
            for k in range(KC):
                for sl in range(4):
                    ins = tensor.matmul(out=qbank(oc, sl),
                                        lhsT=w_bf[oc % 2][:, k * 128:(k + 1) * 128],
                                        rhs=scf[:, k * S + sl * 512: k * S + (sl + 1) * 512],
                                        start=(k == 0), stop=(k == KC - 1))
                    if k == KC - 1:
                        ins.then_inc(sem["t_q"], 1)
        for k2 in range(KC):
            tensor.wait_ge(sem["v_cast"], vcc(8 + k2))
            tensor.wait_ge(sem["s_qf"], max(32, 28 + 4 * k2))
            for h in range(KC):
                for sl in range(4):
                    ins = tensor.matmul(out=qbank(k2, sl),
                                        lhsT=w_bf[k2 % 2][:, h * 128:(h + 1) * 128],
                                        rhs=qf[:, h * S + sl * 512: h * S + (sl + 1) * 512],
                                        start=(h == 0), stop=(h == KC - 1))
                    if h == KC - 1:
                        ins.then_inc(sem["t_q"], 1)

        tensor.wait_ge(sem["s_qf"], 64)
        tensor.wait_ge(sem["v_vt"], NT)
        tensor.wait_ge(sem["v_wb"], 8)

        sbank = [psA, psB, psC, psD]

        def scores_mms(j):
            for sl in range(4):
                if j >= 1:
                    tensor.wait_ge(sem["v_scev"], 4 * (j - 1) + sl + 1)
                if sl < 2 and j >= 3:
                    tensor.wait_ge(sem["s_lF"], 2 * (j - 3) + sl + 1)
                for k in range(KC):
                    ins = tensor.matmul(
                        out=sbank[sl][:],
                        lhsT=qf[:, k * S + j * 128: k * S + (j + 1) * 128],
                        rhs=tqf[:, k * S + sl * 512: k * S + (sl + 1) * 512],
                        start=(k == 0), stop=(k == KC - 1))
                ins.then_inc(sem["t_sc"], 1)

        scores_mms(0)
        for i in range(NT):
            if i < NT - 1:
                scores_mms(i + 1)
            tensor.wait_ge(sem["s_exp"], i + 1)
            if i >= 1:
                tensor.wait_ge(sem["v_ctxev"], i)
            if i >= 2:
                tensor.wait_ge(sem["v_ffnev"], i - 1)
            for f in range(4):
                dst = psE if f < 2 else psG
                half = (f % 2) * 512
                for jj in range(4):
                    j = f * 4 + jj
                    ins = tensor.matmul(out=dst[:, half + jj * 128: half + (jj + 1) * 128],
                                        lhsT=probs[i % 2][:, j * 128:(j + 1) * 128],
                                        rhs=ident[:], start=True, stop=True)
                ins.then_inc(sem["t_pT"], 1)
            if i >= 1:
                tensor.wait_ge(sem["v_ln2e"], i)
                for f in range(2):
                    if i < NT - 1:
                        tensor.wait_ge(sem["v_scev"], 4 * (i + 1) + f + 1)
                    else:
                        tensor.wait_ge(sem["s_lF"], 2 * (i - 2) + f + 1)
                    for kk in range(4):
                        k = f * 4 + kk
                        ins = tensor.matmul(out=psAB[f][:, kk * 128:(kk + 1) * 128],
                                            lhsT=ln2tok[(i - 1) % 2][:, k * 128:(k + 1) * 128],
                                            rhs=ident[:], start=True, stop=True)
                    ins.then_inc(sem["t_lT"], 1)
            tensor.wait_ge(sem["s_pT"], 4 * i + 4)
            for j in range(NT):
                if False:
                    pass
                tensor.matmul(out=psE[:, 0:512],
                              lhsT=probsT[:, j * 128:(j + 1) * 128],
                              rhs=vt[:, j * D: j * D + 512],
                              start=(j == 0), stop=(j == NT - 1))
                ins = tensor.matmul(out=psE[:, 512:1024],
                                    lhsT=probsT[:, j * 128:(j + 1) * 128],
                                    rhs=vt[:, j * D + 512: (j + 1) * D],
                                    start=(j == 0), stop=(j == NT - 1))
            ins.then_inc(sem["t_ctx"], 1)
            if i >= 1:
                tensor.wait_ge(sem["s_lF"], 2 * i)
                tensor.wait_ge(sem["v_ffnev"], i - 1)
                for k in range(KC):
                    tensor.matmul(out=psG[:, 0:512],
                                  lhsT=ln2F[(i - 1) % 2][:, k * 128:(k + 1) * 128],
                                  rhs=wbig[:, k * D: k * D + 512],
                                  start=(k == 0), stop=(k == KC - 1))
                    ins = tensor.matmul(out=psG[:, 512:1024],
                                        lhsT=ln2F[(i - 1) % 2][:, k * 128:(k + 1) * 128],
                                        rhs=wbig[:, k * D + 512: (k + 1) * D],
                                        start=(k == 0), stop=(k == KC - 1))
                ins.then_inc(sem["t_ffn"], 1)
        i = NT
        tensor.wait_ge(sem["v_ln2e"], NT)
        for f in range(2):
            tensor.wait_ge(sem["s_lF"], 2 * (NT - 2) + f + 1)
            for kk in range(4):
                k = f * 4 + kk
                ins = tensor.matmul(out=psAB[f][:, kk * 128:(kk + 1) * 128],
                                    lhsT=ln2tok[(NT - 1) % 2][:, k * 128:(k + 1) * 128],
                                    rhs=ident[:], start=True, stop=True)
            ins.then_inc(sem["t_lT"], 1)
        tensor.wait_ge(sem["s_lF"], 2 * NT)
        tensor.wait_ge(sem["v_ffnev"], NT - 1)
        for k in range(KC):
            tensor.matmul(out=psG[:, 0:512],
                          lhsT=ln2F[(NT - 1) % 2][:, k * 128:(k + 1) * 128],
                          rhs=wbig[:, k * D: k * D + 512],
                          start=(k == 0), stop=(k == KC - 1))
            ins = tensor.matmul(out=psG[:, 512:1024],
                                lhsT=ln2F[(NT - 1) % 2][:, k * 128:(k + 1) * 128],
                                rhs=wbig[:, k * D + 512: (k + 1) * D],
                                start=(k == 0), stop=(k == KC - 1))
        ins.then_inc(sem["t_ffn"], 1)

    @block.vector
    def _(vector):
        vector.memset(epsc[:], EPS)
        vector.wait_ge(sem["t_bc"], 1)
        vector.tensor_copy(out=bv_bc[:], in_=psv[0][:]).then_inc(sem["v_bc"], 1)
        vector.tensor_copy(out=bf_bc[:], in_=psv[1][:]).then_inc(sem["v_bc"], 1)
        def wv_cast(j):
            vector.wait_ge(*wsem(16 + j))
            vector.tensor_copy(out=wv_bf[:, j * D:(j + 1) * D], in_=wv_f32[j % 2][:]
                               ).then_inc(sem["v_cast"], 1)

        for t in range(NT):
            vector.wait_ge(*xsem(t))
            vector.bn_stats(out=bn12[:, 0:6], in_=x_t[t % 4][:, 0:512])
            vguard(vector, vector.bn_stats(out=bn12[:, 6:12], in_=x_t[t % 4][:, 512:1024]))
            vector.bn_aggr(out=mv_all[:, 2 * t:2 * t + 2], in_=bn12[:]
                           ).then_inc(sem["v_r0"], 1)
            if t < KC:
                wv_cast(t)
        vector.wait_ge(sem["s_inv0"], 1)
        vguard(vector, vector.tensor_tensor(out=norm_all[:], in0=sa16[:],
                                            in1=inv16[:], op=ALU.mult))
        vguard(vector, vector.tensor_scalar(out=vtmp[:], in0=norm_all[:],
                                            scalar1=NEPS, scalar2=None, op0=ALU.add))
        vguard(vector, vector.reciprocal(out=rn16[:], in_=vtmp[:]))
        vguard(vector, vector.tensor_tensor(out=k16[:], in0=inv16[:], in1=rn16[:],
                                            op=ALU.mult))
        for t in range(NT):
            vector.wait_ge(*xsemB(t))
            if t >= 1:
                vector.wait_ge(sem["t_scT"], 2 * t)
            vector.tensor_scalar(out=sctok[t % 2][:], in0=x_t[t % 4][:],
                                 scalar1=mv_all[:, 2 * t:2 * t + 1], scalar2=k16[:, t:t + 1],
                                 op0=ALU.subtract, op1=ALU.mult
                                 ).then_inc(sem["v_emit"], 1)
            if t >= 2:
                vector.wait_ge(sem["t_q"], t - 1)
                vector.tensor_tensor(out=vt[:, (t - 2) * D:(t - 1) * D], in0=psv[t % 2][:],
                                     in1=bv_bc[:], op=ALU.add).then_inc(sem["v_vt"], 1)
        for tc in range(NT - 2, NT):
            vector.wait_ge(sem["t_q"], tc + 1)
            vector.tensor_tensor(out=vt[:, tc * D:(tc + 1) * D], in0=psv[tc % 2][:],
                                 in1=bv_bc[:], op=ALU.add).then_inc(sem["v_vt"], 1)
        def wbuild_unit(u):
            m, chunk = u // 4, u % 4
            for c in range(4):
                n = m * 16 + chunk * 4 + c
                vector.wait_ge(*e1sem(n))
                vector.wait_ge(sem["ld_e2"], 64 * (m + 1))
                if c == 0:
                    vector.tensor_tensor(out=wacc[:], in0=e1_t[n % 2][:],
                                         in1=e2_sb[:, c * 512:(c + 1) * 512],
                                         op=ALU.mult).then_inc(sem["v_wbm"], 1)
                else:
                    vector.tensor_tensor(out=wtmp[:], in0=e1_t[n % 2][:],
                                         in1=e2_sb[:, c * 512:(c + 1) * 512],
                                         op=ALU.mult).then_inc(sem["v_wbm"], 1)
                    vector.tensor_tensor(out=wacc[:], in0=wacc[:], in1=wtmp[:],
                                         op=ALU.add)
            if m == 0:
                vector.tensor_scalar(out=wbig[:, chunk * D: chunk * D + 512],
                                     in0=wacc[:], scalar1=1.0, scalar2=None,
                                     op0=ALU.mult)
                vector.tensor_scalar(out=wbig[:, (chunk + 4) * D + 512: (chunk + 5) * D],
                                     in0=wacc[:], scalar1=1.0, scalar2=None,
                                     op0=ALU.mult).then_inc(sem["v_wb"], 1)
            else:
                vector.tensor_scalar(out=wbig[:, chunk * D + 512: (chunk + 1) * D],
                                     in0=wacc[:], scalar1=1.0, scalar2=None,
                                     op0=ALU.mult)
                vector.tensor_scalar(out=wbig[:, (chunk + 4) * D: (chunk + 4) * D + 512],
                                     in0=wacc[:], scalar1=-1.0, scalar2=None,
                                     op0=ALU.mult).then_inc(sem["v_wb"], 1)

        for s_ in range(16):
            vector.wait_ge(*wsem(s_))
            if s_ >= 2:
                vector.wait_ge(sem["t_q"], 16 + 4 * (s_ - 1))
            vector.tensor_copy(out=w_bf[s_ % 2][:], in_=w_f32[0][:]
                               ).then_inc(sem["v_cast"], 1)
            if 2 <= s_ <= 9:
                wbuild_unit(s_ - 2)
        vector.wait_ge(sem["v_scev"], 4)
        vector.reduce_max(out=negm_all[:, 0:1], in_=sc_sb[0][:],
                          axis=AX.X, negate=True).then_inc(sem["v_max"], 1)
        for i in range(NT):
            vector.wait_ge(sem["s_exp"], i + 1)
            vguard(vector, vector.reciprocal(out=rs_s[:], in_=sume_all[:, i:i + 1]))
            vguard(vector, vector.tensor_tensor(out=csc[:], in0=rs_s[:],
                                                in1=norm_all[:, i:i + 1], op=ALU.mult))
            if i < NT - 1:
                vector.wait_ge(sem["v_scev"], 4 * (i + 1) + 4)
                vector.reduce_max(out=negm_all[:, i + 1:i + 2], in_=sc_sb[(i + 1) % 2][:],
                                  axis=AX.X, negate=True).then_inc(sem["v_max"], 1)
            vector.wait_ge(sem["t_ctx"], i + 1)
            vector.tensor_scalar(out=attn[:], in0=psE[:], scalar1=csc[:], scalar2=None,
                                 op0=ALU.mult).then_inc(sem["v_ctxev"], 1)
            if i >= 1:
                vector.wait_ge(sem["t_ffn"], i)
                vector.tensor_tensor(out=ffn_f[:], in0=psG[:], in1=bf_bc[:],
                                     op=ALU.add).then_inc(sem["v_ffnev"], 1)
            vector.bn_stats(out=bn12[:, 0:6], in_=attn[:, 0:512])
            vguard(vector, vector.bn_stats(out=bn12[:, 6:12], in_=attn[:, 512:1024]))
            vector.bn_aggr(out=mvA[:], in_=bn12[:]).then_inc(sem["v_rA"], 1)
            vector.wait_ge(sem["s_invA"], i + 1)
            vector.tensor_scalar(out=ytmp[:], in0=attn[:], scalar1=mvA[:, 0:1],
                                 scalar2=invA[:], op0=ALU.subtract, op1=ALU.mult)
            vector.wait_ge(*xsem2(i))
            vector.tensor_tensor(out=y_buf[i % 3][:], in0=ytmp[:], in1=x_t[i % 4][:],
                                 op=ALU.add).then_inc(sem["v_y"], 1)
            vector.bn_stats(out=bn12[:, 0:6], in_=y_buf[i % 3][:, 0:512])
            vguard(vector, vector.bn_stats(out=bn12[:, 6:12], in_=y_buf[i % 3][:, 512:1024]))
            vector.bn_aggr(out=mvL[:], in_=bn12[:]).then_inc(sem["v_rL"], 1)
            vector.wait_ge(sem["s_invL"], i + 1)
            if i >= 2:
                vector.wait_ge(sem["t_lT"], 2 * (i - 2) + 2)
            vector.tensor_scalar(out=ln2tok[i % 2][:], in0=y_buf[i % 3][:],
                                 scalar1=mvL[:, 0:1], scalar2=invL[:],
                                 op0=ALU.subtract, op1=ALU.mult
                                 ).then_inc(sem["v_ln2e"], 1)
            if i >= 1:
                vector.bn_stats(out=bn12[:, 0:6], in_=ffn_f[:, 0:512])
                vguard(vector, vector.bn_stats(out=bn12[:, 6:12], in_=ffn_f[:, 512:1024]))
                vector.bn_aggr(out=mvF[:], in_=bn12[:]).then_inc(sem["v_rF"], 1)
                vector.wait_ge(sem["s_invF"], i)
                vector.tensor_scalar(out=ytmp[:], in0=ffn_f[:], scalar1=mvF[:, 0:1],
                                     scalar2=invF[:], op0=ALU.subtract, op1=ALU.mult)
                if i - 1 >= 2:
                    vector.wait_ge(*stsem(i - 3))
                vector.tensor_tensor(out=out_b[(i - 1) % 2][:], in0=ytmp[:],
                                     in1=y_buf[(i - 1) % 3][:], op=ALU.add
                                     ).then_inc(sem["v_out"], 1)
        vector.wait_ge(sem["t_ffn"], NT)
        vector.tensor_tensor(out=ffn_f[:], in0=psG[:], in1=bf_bc[:],
                             op=ALU.add).then_inc(sem["v_ffnev"], 1)
        vector.bn_stats(out=bn12[:, 0:6], in_=ffn_f[:, 0:512])
        vguard(vector, vector.bn_stats(out=bn12[:, 6:12], in_=ffn_f[:, 512:1024]))
        vector.bn_aggr(out=mvF[:], in_=bn12[:]).then_inc(sem["v_rF"], 1)
        vector.wait_ge(sem["s_invF"], NT)
        vector.tensor_scalar(out=ytmp[:], in0=ffn_f[:], scalar1=mvF[:, 0:1],
                             scalar2=invF[:], op0=ALU.subtract, op1=ALU.mult)
        vector.wait_ge(*stsem(NT - 3))
        vector.tensor_tensor(out=out_b[(NT - 1) % 2][:], in0=ytmp[:],
                             in1=y_buf[(NT - 1) % 3][:], op=ALU.add
                             ).then_inc(sem["v_out"], 1)

    @block.scalar
    def _(scalar):
        for t in range(1, NT, 2):
            if t >= 4:
                scalar.wait_ge(sem["v_r0"], t - 3)
            scalar.dma_start(out=x_t[t % 4][:], in_=x_hbm[t * 128:(t + 1) * 128, :]
                             ).then_inc(xsem(t)[0], 16)
        scalar.wait_ge(sem["t_bc"], 1)
        scalar.dma_start(out=w_f32[0][:], in_=wq_hbm[0]).then_inc(sem["ld_w0"], 16)
        scalar.wait_ge(sem["ld_c"], 16 * 5)
        scalar.wait_ge(sem["v_r0"], NT)
        var16 = mv_all[:].rearrange("p (t two) -> p two t", two=2)[:, 1, :]
        sguard(scalar, scalar.activation(out=lns16[:], in_=var16, func=AF.Ln, bias=epsc[:]))
        scalar.activation(out=inv16[:], in_=lns16[:], func=AF.Exp, scale=-0.5)
        sguard(scalar, scalar.activation(out=lns16[:], in_=var16, func=AF.Ln, scale=1024.0))
        scalar.activation(out=sa16[:], in_=lns16[:], func=AF.Exp, scale=0.5
                          ).then_inc(sem["s_inv0"], 1)
        for t in range(NT):
            scf4 = scf[:].rearrange("p (k t j) -> p k t j", k=KC, t=NT, j=128)
            scalar.wait_ge(sem["t_scT"], 2 * t + 1)
            scalar.copy(out=scf4[:, 0:4, t, :],
                        in_=psq[0][:].rearrange("p (k j) -> p k j", k=4)
                        ).then_inc(sem["s_scF"], 1)
            scalar.wait_ge(sem["t_scT"], 2 * t + 2)
            scalar.copy(out=scf4[:, 4:8, t, :],
                        in_=psq[1][:].rearrange("p (k j) -> p k j", k=4)
                        ).then_inc(sem["s_scF"], 1)
        scalar.wait_ge(sem["v_cast"], vcc(0))
        scalar.dma_start(out=w_f32[0][:], in_=wq_hbm[1]).then_inc(sem["ld_w1"], 16)
        for oc in range(KC):
            if oc + 2 < KC:
                scalar.wait_ge(sem["v_cast"], vcc(oc + 1))
                scalar.dma_start(out=w_f32[0][:], in_=wq_hbm[oc + 2]
                                 ).then_inc(wsem(oc + 2)[0], 16)
            elif oc + 2 < KC + 2:
                scalar.wait_ge(sem["v_cast"], vcc(oc + 1))
                scalar.dma_start(out=w_f32[0][:], in_=am_hbm[oc + 2 - KC]
                                 ).then_inc(wsem(oc + 2)[0], 16)
            for sl in range(4):
                scalar.wait_ge(sem["t_q"], 16 + 4 * oc + sl + 1)
                src_b = (psq[sl][:] if oc % 2 == 0
                         else psv[sl // 2][:, (sl % 2) * 512:(sl % 2) * 512 + 512])
                scalar.add(out=qf[:, oc * S + sl * 512: oc * S + (sl + 1) * 512],
                           in_=src_b, add=bq_t[:, oc:oc + 1]).then_inc(sem["s_qf"], 1)
        for k2 in range(KC):
            if k2 + 2 < KC:
                scalar.wait_ge(sem["v_cast"], vcc(9 + k2))
                scalar.dma_start(out=w_f32[0][:], in_=am_hbm[k2 + 2]
                                 ).then_inc(wsem(k2 + 10)[0], 16)
            for sl in range(4):
                scalar.wait_ge(sem["t_q"], 48 + 4 * k2 + sl + 1)
                src_b = (psq[sl][:] if k2 % 2 == 0
                         else psv[sl // 2][:, (sl % 2) * 512:(sl % 2) * 512 + 512])
                scalar.copy(out=tqf[:, k2 * S + sl * 512: k2 * S + (sl + 1) * 512],
                            in_=src_b).then_inc(sem["s_qf"], 1)
        sbank_s = [psA, psB, psC, psD]

        def sc_evac(j):
            for sl in range(4):
                g = 4 * j + sl
                scalar.wait_ge(sem["t_sc"], g + 1)
                scalar.mul(out=sc_sb[j % 2][:, sl * 512:(sl + 1) * 512],
                           in_=sbank_s[sl][:], mul=ISQ).then_inc(sem["v_scev"], 1)

        def s_exp(j):
            scalar.wait_ge(sem["v_max"], j + 1)
            if j >= 2:
                scalar.wait_ge(sem["t_pT"], 4 * (j - 2) + 4)
            scalar.activation(out=probs[j % 2][:], in_=sc_sb[j % 2][:], func=AF.Exp,
                              bias=negm_all[:, j:j + 1],
                              accum_out=sume_all[:, j:j + 1]).then_inc(sem["s_exp"], 1)

        def sc_evac_part(j, sls):
            for sl in sls:
                g = 4 * j + sl
                scalar.wait_ge(sem["t_sc"], g + 1)
                scalar.mul(out=sc_sb[j % 2][:, sl * 512:(sl + 1) * 512],
                           in_=sbank_s[sl][:], mul=ISQ).then_inc(sem["v_scev"], 1)

        def s_invL(j):
            scalar.wait_ge(sem["v_rL"], j + 1)
            sguard(scalar, scalar.activation(out=lns[:], in_=mvL[:, 1:2], func=AF.Ln, bias=epsc[:]))
            scalar.activation(out=invL[:], in_=lns[:], func=AF.Exp, scale=-0.5
                              ).then_inc(sem["s_invL"], 1)

        def s_invF(j):
            scalar.wait_ge(sem["v_rF"], j + 1)
            sguard(scalar, scalar.activation(out=lns[:], in_=mvF[:, 1:2], func=AF.Ln, bias=epsc[:]))
            scalar.activation(out=invF[:], in_=lns[:], func=AF.Exp, scale=-0.5
                              ).then_inc(sem["s_invF"], 1)

        def s_store(j):
            scalar.wait_ge(sem["v_out"], j + 1)
            if j >= 2:
                scalar.wait_ge(*stsem(j - 2))
            scalar.dma_start(out=y_hbm[j * 128: (j + 1) * 128, :],
                             in_=out_b[j % 2][:]).then_inc(stsem(j)[0], 16)

        sc_evac(0)
        s_exp(0)
        for i in range(NT):
            if i < NT - 1:
                scalar.wait_ge(sem["v_max"], i + 1)
                sc_evac_part(i + 1, (0, 1))
            if i >= 1:
                s_invL(i - 1)
            if i < NT - 1:
                sc_evac_part(i + 1, (2, 3))
            if i >= 1:
                scalar.wait_ge(sem["t_ctx"], i)
            for f in range(4):
                scalar.wait_ge(sem["t_pT"], 4 * i + f + 1)
                srcp = psE if f < 2 else psG
                scalar.copy(out=probsT[:, f * 512:(f + 1) * 512],
                            in_=srcp[:, (f % 2) * 512:(f % 2) * 512 + 512]
                            ).then_inc(sem["s_pT"], 1)
            if i >= 1:
                if i >= 2:
                    scalar.wait_ge(sem["t_ffn"], i - 1)
                for f in range(2):
                    scalar.wait_ge(sem["t_lT"], 2 * (i - 1) + f + 1)
                    scalar.copy(out=ln2F[(i - 1) % 2][:, f * 512:(f + 1) * 512],
                                in_=psAB[f][:]).then_inc(sem["s_lF"], 1)
            if i >= 2:
                s_invF(i - 2)
            if i < NT - 1:
                s_exp(i + 1)
            if i >= 2:
                s_store(i - 2)
            scalar.wait_ge(sem["v_rA"], i + 1)
            sguard(scalar, scalar.activation(out=lns[:], in_=mvA[:, 1:2], func=AF.Ln, bias=epsc[:]))
            scalar.activation(out=invA[:], in_=lns[:], func=AF.Exp, scale=-0.5
                              ).then_inc(sem["s_invA"], 1)
        s_invL(NT - 1)
        for f in range(2):
            scalar.wait_ge(sem["t_lT"], 2 * (NT - 1) + f + 1)
            scalar.copy(out=ln2F[(NT - 1) % 2][:, f * 512:(f + 1) * 512],
                        in_=psAB[f][:]).then_inc(sem["s_lF"], 1)
        s_invF(NT - 2)
        s_store(NT - 2)
        s_invF(NT - 1)
        s_store(NT - 1)
        scalar.wait_ge(sem["st0"], 16 * 8)
        scalar.wait_ge(sem["st1"], 16 * 8)

    if debug:
        @block.gpsimd
        def _(gpsimd):
            gpsimd.wait_ge(sem["st0"], 16 * 8)
            gpsimd.wait_ge(sem["st1"], 16 * 8)
            gpsimd.dma_start(out=dbg["d_qf"], in_=qf[:]).then_inc(sem["dbg"], 16)
            gpsimd.dma_start(out=dbg["d_tqf"], in_=tqf[:]).then_inc(sem["dbg"], 16)
            gpsimd.dma_start(out=dbg["d_vt"], in_=vt[:]).then_inc(sem["dbg"], 16)
            gpsimd.dma_start(out=dbg["d_wbig"], in_=wbig[:]).then_inc(sem["dbg"], 16)
            gpsimd.dma_start(out=dbg["d_norm"], in_=norm_all[:]).then_inc(sem["dbg"], 16)
            gpsimd.dma_start(out=dbg["d_sums"], in_=sume_all[:]).then_inc(sem["dbg"], 16)
            gpsimd.wait_ge(sem["dbg"], 16 * 6)

    es.close()
    return nc


def _tile8(w):
    return np.ascontiguousarray(
        np.transpose(w.reshape(8, 128, 8, 128), (2, 1, 0, 3)).reshape(8, 128, 1024))


def stage_inputs(L0_attn_hidden_states, L0_attn_Wq, L0_attn_bq, L0_attn_Wv, L0_attn_bv,
                 L0_attn_anomaly_matrix, L0_ffn_G1_R, L0_ffn_G2_R, L0_ffn_G1_I,
                 L0_ffn_G2_I, L0_ffn_bias_real, L0_ffn_bias_imag):
    f32 = np.float32
    x = np.asarray(L0_attn_hidden_states, dtype=f32)
    Wq = np.asarray(L0_attn_Wq, dtype=f32)
    Wv = np.asarray(L0_attn_Wv, dtype=f32)
    A = np.asarray(L0_attn_anomaly_matrix, dtype=f32)
    bq = np.asarray(L0_attn_bq, dtype=f32)
    bv = np.asarray(L0_attn_bv, dtype=f32)
    G1R = np.asarray(L0_ffn_G1_R, dtype=f32)
    G2R = np.asarray(L0_ffn_G2_R, dtype=f32)
    G1I = np.asarray(L0_ffn_G1_I, dtype=f32)
    G2I = np.asarray(L0_ffn_G2_I, dtype=f32)
    br = np.asarray(L0_ffn_bias_real, dtype=f32)
    bi = np.asarray(L0_ffn_bias_imag, dtype=f32)

    common = dict(
        wq_t=_tile8(np.ascontiguousarray(Wq.T)),
        amat_t=_tile8(A),
        wv_t=np.ascontiguousarray(Wv.T).reshape(8, 128, 1024),
        bq_t=np.ascontiguousarray(bq.reshape(8, 128).T),
        bv_row=bv.reshape(1, 1024),
        brbi_row=np.concatenate([br, bi]).reshape(1, 1024),
        ones_row=np.ones((1, 128), dtype=f32),
        ident=np.eye(128, dtype=ml_dtypes.bfloat16),
        e1=np.stack([
            np.stack([np.repeat(np.repeat(np.ascontiguousarray(G1[0, :, c, :].T), 16, 0),
                                16, 1) for c in range(4)])
            for G1 in (G1R, G1I)]).astype(f32),
        e2=np.stack([
            np.stack([np.tile(np.ascontiguousarray(G2[c, :, 0, :].T), (8, 32))
                      for c in range(4)])
            for G2 in (G2R, G2I)]).astype(f32),
    )
    in_maps = [dict(common, x=np.ascontiguousarray(x[b])) for b in range(8)]
    return in_maps


def kernel(**inputs):
    in_maps = stage_inputs(**inputs)
    nc = build_nc()
    res = run_bass_kernel_spmd(nc, in_maps, core_ids=list(range(8)))
    return np.stack([res.results[c]["y_out"] for c in range(8)], axis=0)


# version 17
# speedup vs baseline: 1.0300x; 1.0096x over previous
import numpy as np
import ml_dtypes
from contextlib import ExitStack
import concourse.bass as bass
import concourse.mybir as mybir
from concourse.bass_utils import run_bass_kernel_spmd

F32 = mybir.dt.float32
BF16 = mybir.dt.bfloat16
AF = mybir.ActivationFunctionType
ALU = mybir.AluOpType
AX = mybir.AxisListType

D = 1024
S = 2048
NT = 16
KC = 8
EPS = 1e-5
NEPS = 1e-9
ISQ = 1.0 / 32.0


def build_nc(debug=False):
    nc = bass.Bass("TRN2", target_bir_lowering=False, detect_race_conditions=False)

    x_hbm = nc.dram_tensor("x", [S, D], F32, kind="ExternalInput").ap()
    wq_hbm = nc.dram_tensor("wq_t", [KC, 128, D], F32, kind="ExternalInput").ap()
    am_hbm = nc.dram_tensor("amat_t", [KC, 128, D], F32, kind="ExternalInput").ap()
    wv_hbm = nc.dram_tensor("wv_t", [KC, 128, D], F32, kind="ExternalInput").ap()
    bq_hbm = nc.dram_tensor("bq_t", [128, KC], F32, kind="ExternalInput").ap()
    bv_hbm = nc.dram_tensor("bv_row", [1, D], F32, kind="ExternalInput").ap()
    bf_hbm = nc.dram_tensor("brbi_row", [1, D], F32, kind="ExternalInput").ap()
    on_hbm = nc.dram_tensor("ones_row", [1, 128], F32, kind="ExternalInput").ap()
    id_hbm = nc.dram_tensor("ident", [128, 128], BF16, kind="ExternalInput").ap()
    e1_hbm = nc.dram_tensor("e1", [2, 4, 2, 128, 1024], F32, kind="ExternalInput").ap()
    e2_hbm = nc.dram_tensor("e2", [2, 4, 128, 512], F32, kind="ExternalInput").ap()

    y_hbm = nc.dram_tensor("y_out", [S, D], F32, kind="ExternalOutput").ap()
    if debug:
        dbg = {
            "d_qf": nc.dram_tensor("d_qf", [128, KC * S], BF16, kind="ExternalOutput").ap(),
            "d_tqf": nc.dram_tensor("d_tqf", [128, KC * S], BF16, kind="ExternalOutput").ap(),
            "d_vt": nc.dram_tensor("d_vt", [128, NT * D], BF16, kind="ExternalOutput").ap(),
            "d_wbig": nc.dram_tensor("d_wbig", [128, KC * D], BF16, kind="ExternalOutput").ap(),
            "d_norm": nc.dram_tensor("d_norm", [128, NT], F32, kind="ExternalOutput").ap(),
            "d_sums": nc.dram_tensor("d_sums", [128, NT], F32, kind="ExternalOutput").ap(),
        }

    es = ExitStack()

    qf = es.enter_context(nc.sbuf_tensor("qf", [128, KC * S], BF16))
    tqf = es.enter_context(nc.sbuf_tensor("tqf", [128, KC * S], BF16))
    vt = es.enter_context(nc.sbuf_tensor("vt", [128, NT * D], BF16))
    wbig = es.enter_context(nc.sbuf_tensor("wbig", [128, KC * D], BF16))
    x_t = [es.enter_context(nc.sbuf_tensor(f"x_t{i}", [128, D], F32)) for i in range(4)]
    bq_t = es.enter_context(nc.sbuf_tensor("sb_bq", [128, KC], F32))
    ident = es.enter_context(nc.sbuf_tensor("sb_id", [128, 128], BF16))
    bv_bc = es.enter_context(nc.sbuf_tensor("bv_bc", [128, D], F32))
    bf_bc = es.enter_context(nc.sbuf_tensor("bf_bc", [128, D], F32))
    norm_all = es.enter_context(nc.sbuf_tensor("norm_all", [128, NT], F32))
    negm_all = es.enter_context(nc.sbuf_tensor("negm_all", [128, NT], F32))
    sume_all = es.enter_context(nc.sbuf_tensor("sume_all", [128, NT], F32))
    bn12 = es.enter_context(nc.sbuf_tensor("bn12", [128, 12], F32))
    mv_all = es.enter_context(nc.sbuf_tensor("mv_all", [128, 2 * NT], F32))
    inv16 = es.enter_context(nc.sbuf_tensor("inv16", [128, NT], F32))
    sa16 = es.enter_context(nc.sbuf_tensor("sa16", [128, NT], F32))
    k16 = es.enter_context(nc.sbuf_tensor("k16", [128, NT], F32))
    vtmp = es.enter_context(nc.sbuf_tensor("vtmp", [128, NT], F32))
    epsc = es.enter_context(nc.sbuf_tensor("epsc", [128, 1], F32))
    rn16 = es.enter_context(nc.sbuf_tensor("rn16", [128, NT], F32))
    lns16 = es.enter_context(nc.sbuf_tensor("lns16", [128, NT], F32))
    lns = es.enter_context(nc.sbuf_tensor("lns", [128, 1], F32))

    es1 = ExitStack()
    scf = es1.enter_context(nc.sbuf_tensor("scf", [128, KC * S], BF16))
    sctok = [es1.enter_context(nc.sbuf_tensor("sctok0", [128, D], BF16))] * 2
    es1r = ExitStack()
    bv_row = es1r.enter_context(nc.sbuf_tensor("sb_bvr", [1, D], F32))
    bf_row = es1r.enter_context(nc.sbuf_tensor("bf_row", [1, D], F32))
    ones_r = es1r.enter_context(nc.sbuf_tensor("ones_r", [1, 128], F32))
    es1r.close()
    wv_f32 = [es1.enter_context(nc.sbuf_tensor(f"wv_f32_{i}", [128, D], F32)) for i in range(2)]
    es1a = ExitStack()
    e1_t = [es1a.enter_context(nc.sbuf_tensor(f"e1_t{i}", [128, 1024], F32)) for i in range(2)]
    e2_sb = es1a.enter_context(nc.sbuf_tensor("e2_sb", [128, 2048], F32))
    wacc = es1a.enter_context(nc.sbuf_tensor("wacc", [128, 512], F32))
    wtmp = es1a.enter_context(nc.sbuf_tensor("wtmp", [128, 512], F32))
    w_f32 = [es1a.enter_context(nc.sbuf_tensor("w_f32_0", [128, D], F32))] * 2
    w_bf = [es1a.enter_context(nc.sbuf_tensor(f"w_bf_{i}", [128, D], BF16)) for i in range(2)]
    es1a.close()
    es1b = ExitStack()
    wv_bf = es1b.enter_context(nc.sbuf_tensor("wv_bf", [128, KC * D], BF16))
    es1b.close()
    es1.close()

    es2 = ExitStack()
    sc_sb = [es2.enter_context(nc.sbuf_tensor(f"sc_sb{i}", [128, S], F32)) for i in range(2)]
    probs = [es2.enter_context(nc.sbuf_tensor(f"probs{i}", [128, S], BF16)) for i in range(2)]
    probsT = es2.enter_context(nc.sbuf_tensor("probsT", [128, S], BF16))
    attn = es2.enter_context(nc.sbuf_tensor("attn", [128, D], F32))
    y_buf = [es2.enter_context(nc.sbuf_tensor(f"y_buf{i}", [128, D], F32)) for i in range(3)]
    ln2tok = [es2.enter_context(nc.sbuf_tensor(f"ln2tok{i}", [128, D], BF16)) for i in range(2)]
    ln2F = [es2.enter_context(nc.sbuf_tensor(f"ln2F{i}", [128, D], BF16)) for i in range(2)]
    ffn_f = es2.enter_context(nc.sbuf_tensor("ffn_f", [128, D], F32))
    out_b = [es2.enter_context(nc.sbuf_tensor(f"out_b{i}", [128, D], F32)) for i in range(2)]
    mvA = es2.enter_context(nc.sbuf_tensor("mvA", [128, 2], F32))
    mvL = es2.enter_context(nc.sbuf_tensor("mvL", [128, 2], F32))
    mvF = es2.enter_context(nc.sbuf_tensor("mvF", [128, 2], F32))
    invA = es2.enter_context(nc.sbuf_tensor("invA", [128, 1], F32))
    invL = es2.enter_context(nc.sbuf_tensor("invL", [128, 1], F32))
    invF = es2.enter_context(nc.sbuf_tensor("invF", [128, 1], F32))
    rs_s = es2.enter_context(nc.sbuf_tensor("rs_s", [128, 1], F32))
    csc = es2.enter_context(nc.sbuf_tensor("csc", [128, 1], F32))
    ytmp = es2.enter_context(nc.sbuf_tensor("ytmp", [128, D], F32))
    es2.close()

    esp1 = ExitStack()
    psq = [esp1.enter_context(nc.psum_tensor(f"psq{i}", [128, 512], F32)) for i in range(4)]
    psv = [esp1.enter_context(nc.psum_tensor(f"psv{i}", [128, 1024], F32)) for i in range(2)]
    esp1.close()
    esp2 = ExitStack()
    psA = esp2.enter_context(nc.psum_tensor("psA", [128, 512], F32))
    psB = esp2.enter_context(nc.psum_tensor("psB", [128, 512], F32))
    psC = esp2.enter_context(nc.psum_tensor("psC", [128, 512], F32))
    psD = esp2.enter_context(nc.psum_tensor("psD", [128, 512], F32))
    psE = esp2.enter_context(nc.psum_tensor("psE", [128, 1024], F32))
    psG = esp2.enter_context(nc.psum_tensor("psG", [128, 1024], F32))
    esp2.close()
    psAB = [psA, psB]
    psCD = [psC, psD]

    semnames = ("ld_c ld_x0 ld_x1 ld_x2 ld_x3 ld_e1a ld_e1b ld_e2 ld_w0 ld_w1 ld_wv st0 st1 t_bc v_bc v_r0 s_inv0 v_stat v_emit t_scT s_scF "
                "v_cast t_q s_qf v_vt v_wbm v_wb dbg v_g s_g t_sc v_scev v_max s_exp t_pT s_pT "
                "t_lT s_lF v_ln2e t_ctx v_ctxev t_ffn v_ffnev v_y v_out v_rA s_invA v_rL "
                "s_invL v_rF s_invF").split()
    sem = {n: es.enter_context(nc.semaphore(n)) for n in semnames}
    block = es.enter_context(nc.Block())

    XB = [4, 4, 4, 4]

    _vg = [0]
    _sg = [0]

    def vguard(eng, ins):
        _vg[0] += 1
        ins.then_inc(sem["v_g"], 1)
        eng.wait_ge(sem["v_g"], _vg[0])

    def sguard(eng, ins):
        _sg[0] += 1
        ins.then_inc(sem["s_g"], 1)
        eng.wait_ge(sem["s_g"], _sg[0])

    def xsem(t):
        return sem[f"ld_x{t % 4}"], 16 * (t // 4 + 1)

    def xsemB(t):
        return sem[f"ld_x{t % 4}"], 16 * (XB[t % 4] + t // 4 + 1)

    def xsem2(i):
        return sem[f"ld_x{i % 4}"], 16 * (2 * XB[i % 4] + i // 4 + 1)

    XLAST = [13, 14, 15, 16]

    def wsem(s_):
        if s_ >= 16:
            return sem["ld_wv"], 16 * (s_ - 16 + 1)
        return sem[f"ld_w{s_ % 2}"], 16 * (s_ // 2 + 1)

    def vcc(stage):
        return stage + 9

    def e1sem(q_):
        return sem["ld_e1a" if q_ % 2 == 0 else "ld_e1b"], 16 * (q_ // 2 + 1)

    def stsem(j):
        return sem[f"st{j % 2}"], 16 * (j // 2 + 1)


    @block.sync
    def _(sync):
        sync.dma_start(out=ident[:], in_=id_hbm).then_inc(sem["ld_c"], 16)
        sync.dma_start(out=bq_t[:], in_=bq_hbm).then_inc(sem["ld_c"], 16)
        sync.dma_start(out=bv_row[:], in_=bv_hbm).then_inc(sem["ld_c"], 16)
        sync.dma_start(out=bf_row[:], in_=bf_hbm).then_inc(sem["ld_c"], 16)
        sync.dma_start(out=ones_r[:], in_=on_hbm).then_inc(sem["ld_c"], 16)
        for t in range(NT):
            if t % 2 == 0:
                if t >= 4:
                    sync.wait_ge(sem["v_r0"], t - 3)
                sync.dma_start(out=x_t[t % 4][:], in_=x_hbm[t * 128:(t + 1) * 128, :]
                               ).then_inc(xsem(t)[0], 16)
            if 1 <= t <= KC:
                j = t - 1
                if j <= 1:
                    sync.wait_ge(sem["t_bc"], 1)
                else:
                    sync.wait_ge(sem["v_cast"], j - 1)
                sync.dma_start(out=wv_f32[j % 2][:], in_=wv_hbm[j]).then_inc(wsem(16 + j)[0], 16)
        for t in range(NT):
            if t < 4:
                sync.wait_ge(sem["v_r0"], XLAST[t % 4])
            else:
                sync.wait_ge(sem["v_emit"], t - 3)
            sync.dma_start(out=x_t[t % 4][:], in_=x_hbm[t * 128:(t + 1) * 128, :]
                           ).then_inc(xsemB(t)[0], 16)
        sync.wait_ge(sem["t_bc"], 1)
        sync.wait_ge(sem["t_q"], 16)
        for m in range(2):
            if m == 1:
                sync.wait_ge(sem["v_wbm"], 16)
            for c in range(4):
                sync.dma_start(out=e2_sb[:, c * 512:(c + 1) * 512],
                               in_=e2_hbm[m, c]).then_inc(sem["ld_e2"], 16)
            for chunk in range(4):
                for pair in range(2):
                    q_ = m * 8 + chunk * 2 + pair
                    if q_ >= 2:
                        sync.wait_ge(sem["v_wbm"], 2 * q_ - 2)
                    sync.dma_start(out=e1_t[q_ % 2][:],
                                   in_=e1_hbm[m, chunk, pair]
                                   ).then_inc(e1sem(q_)[0], 16)
        for i in range(NT):
            if i <= 3:
                sync.wait_ge(sem["v_emit"], XLAST[i % 4])
            else:
                sync.wait_ge(sem["v_y"], i - 3)
            sync.dma_start(out=x_t[i % 4][:], in_=x_hbm[i * 128:(i + 1) * 128, :]
                           ).then_inc(xsem2(i)[0], 16)

    @block.tensor
    def _(tensor):
        tensor.wait_ge(sem["ld_c"], 16 * 5)
        tensor.matmul(out=psv[0][:, 0:512], lhsT=ones_r[:], rhs=bv_row[:, 0:512],
                      start=True, stop=True)
        tensor.matmul(out=psv[0][:, 512:1024], lhsT=ones_r[:], rhs=bv_row[:, 512:1024],
                      start=True, stop=True)
        tensor.matmul(out=psv[1][:, 0:512], lhsT=ones_r[:], rhs=bf_row[:, 0:512],
                      start=True, stop=True)
        tensor.matmul(out=psv[1][:, 512:1024], lhsT=ones_r[:], rhs=bf_row[:, 512:1024],
                      start=True, stop=True).then_inc(sem["t_bc"], 1)

        def v_group(tc):
            tensor.wait_ge(sem["s_scF"], 2 * tc + 2)
            if tc >= 2:
                tensor.wait_ge(sem["v_vt"], tc - 1)
            for k in range(KC):
                tensor.matmul(out=psv[tc % 2][:, 0:512],
                              lhsT=scf[:, k * S + tc * 128: k * S + (tc + 1) * 128],
                              rhs=wv_bf[:, k * D: k * D + 512],
                              start=(k == 0), stop=(k == KC - 1))
                ins = tensor.matmul(out=psv[tc % 2][:, 512:1024],
                                    lhsT=scf[:, k * S + tc * 128: k * S + (tc + 1) * 128],
                                    rhs=wv_bf[:, k * D + 512: (k + 1) * D],
                                    start=(k == 0), stop=(k == KC - 1))
            ins.then_inc(sem["t_q"], 1)

        tensor.wait_ge(sem["v_bc"], 2)
        for t in range(NT):
            tensor.wait_ge(sem["v_emit"], t + 1)
            if t >= 1:
                tensor.wait_ge(sem["s_scF"], 2 * t - 1)
            for k in range(4):
                ins = tensor.matmul(out=psq[0][:, k * 128:(k + 1) * 128],
                                    lhsT=sctok[t % 2][:, k * 128:(k + 1) * 128],
                                    rhs=ident[:], start=True, stop=True)
            ins.then_inc(sem["t_scT"], 1)
            if t >= 1:
                tensor.wait_ge(sem["s_scF"], 2 * t)
            for k in range(4, 8):
                ins = tensor.matmul(out=psq[1][:, (k - 4) * 128:(k - 3) * 128],
                                    lhsT=sctok[t % 2][:, k * 128:(k + 1) * 128],
                                    rhs=ident[:], start=True, stop=True)
            ins.then_inc(sem["t_scT"], 1)
            if t == 0:
                tensor.wait_ge(sem["v_cast"], 8)
            if t >= 1:
                v_group(t - 1)
        v_group(NT - 1)
        def qbank(oc, sl):
            if oc % 2 == 0:
                return psq[sl][:]
            return psv[sl // 2][:, (sl % 2) * 512:(sl % 2) * 512 + 512]

        tensor.wait_ge(sem["v_vt"], NT)
        for oc in range(KC):
            tensor.wait_ge(sem["v_cast"], vcc(oc))
            if oc >= 2:
                tensor.wait_ge(sem["s_qf"], 4 * oc - 4)
            for k in range(KC):
                for sl in range(4):
                    ins = tensor.matmul(out=qbank(oc, sl),
                                        lhsT=w_bf[oc % 2][:, k * 128:(k + 1) * 128],
                                        rhs=scf[:, k * S + sl * 512: k * S + (sl + 1) * 512],
                                        start=(k == 0), stop=(k == KC - 1))
                    if k == KC - 1:
                        ins.then_inc(sem["t_q"], 1)
        for k2 in range(KC):
            tensor.wait_ge(sem["v_cast"], vcc(8 + k2))
            tensor.wait_ge(sem["s_qf"], max(32, 28 + 4 * k2))
            for h in range(KC):
                for sl in range(4):
                    ins = tensor.matmul(out=qbank(k2, sl),
                                        lhsT=w_bf[k2 % 2][:, h * 128:(h + 1) * 128],
                                        rhs=qf[:, h * S + sl * 512: h * S + (sl + 1) * 512],
                                        start=(h == 0), stop=(h == KC - 1))
                    if h == KC - 1:
                        ins.then_inc(sem["t_q"], 1)

        tensor.wait_ge(sem["s_qf"], 64)
        tensor.wait_ge(sem["v_vt"], NT)
        tensor.wait_ge(sem["v_wb"], 8)

        sbank = [psA, psB, psC, psD]

        def scores_mms(j):
            for sl in range(4):
                if j >= 1:
                    tensor.wait_ge(sem["v_scev"], 4 * (j - 1) + sl + 1)
                if sl < 2 and j >= 3:
                    tensor.wait_ge(sem["s_lF"], 2 * (j - 3) + sl + 1)
                for k in range(KC):
                    ins = tensor.matmul(
                        out=sbank[sl][:],
                        lhsT=qf[:, k * S + j * 128: k * S + (j + 1) * 128],
                        rhs=tqf[:, k * S + sl * 512: k * S + (sl + 1) * 512],
                        start=(k == 0), stop=(k == KC - 1))
                ins.then_inc(sem["t_sc"], 1)

        scores_mms(0)
        for i in range(NT):
            if i < NT - 1:
                scores_mms(i + 1)
            tensor.wait_ge(sem["s_exp"], i + 1)
            if i >= 1:
                tensor.wait_ge(sem["v_ctxev"], i)
            if i >= 2:
                tensor.wait_ge(sem["v_ffnev"], i - 1)
            for f in range(4):
                dst = psE if f < 2 else psG
                half = (f % 2) * 512
                for jj in range(4):
                    j = f * 4 + jj
                    ins = tensor.matmul(out=dst[:, half + jj * 128: half + (jj + 1) * 128],
                                        lhsT=probs[i % 2][:, j * 128:(j + 1) * 128],
                                        rhs=ident[:], start=True, stop=True)
                ins.then_inc(sem["t_pT"], 1)
            if i >= 1:
                tensor.wait_ge(sem["v_ln2e"], i)
                for f in range(2):
                    if i < NT - 1:
                        tensor.wait_ge(sem["v_scev"], 4 * (i + 1) + f + 1)
                    else:
                        tensor.wait_ge(sem["s_lF"], 2 * (i - 2) + f + 1)
                    for kk in range(4):
                        k = f * 4 + kk
                        ins = tensor.matmul(out=psAB[f][:, kk * 128:(kk + 1) * 128],
                                            lhsT=ln2tok[(i - 1) % 2][:, k * 128:(k + 1) * 128],
                                            rhs=ident[:], start=True, stop=True)
                    ins.then_inc(sem["t_lT"], 1)
            tensor.wait_ge(sem["s_pT"], 4 * i + 4)
            for j in range(NT):
                if False:
                    pass
                tensor.matmul(out=psE[:, 0:512],
                              lhsT=probsT[:, j * 128:(j + 1) * 128],
                              rhs=vt[:, j * D: j * D + 512],
                              start=(j == 0), stop=(j == NT - 1))
                ins = tensor.matmul(out=psE[:, 512:1024],
                                    lhsT=probsT[:, j * 128:(j + 1) * 128],
                                    rhs=vt[:, j * D + 512: (j + 1) * D],
                                    start=(j == 0), stop=(j == NT - 1))
            ins.then_inc(sem["t_ctx"], 1)
            if i >= 1:
                tensor.wait_ge(sem["s_lF"], 2 * i)
                tensor.wait_ge(sem["v_ffnev"], i - 1)
                for k in range(KC):
                    tensor.matmul(out=psG[:, 0:512],
                                  lhsT=ln2F[(i - 1) % 2][:, k * 128:(k + 1) * 128],
                                  rhs=wbig[:, k * D: k * D + 512],
                                  start=(k == 0), stop=(k == KC - 1))
                    ins = tensor.matmul(out=psG[:, 512:1024],
                                        lhsT=ln2F[(i - 1) % 2][:, k * 128:(k + 1) * 128],
                                        rhs=wbig[:, k * D + 512: (k + 1) * D],
                                        start=(k == 0), stop=(k == KC - 1))
                ins.then_inc(sem["t_ffn"], 1)
        i = NT
        tensor.wait_ge(sem["v_ln2e"], NT)
        for f in range(2):
            tensor.wait_ge(sem["s_lF"], 2 * (NT - 2) + f + 1)
            for kk in range(4):
                k = f * 4 + kk
                ins = tensor.matmul(out=psAB[f][:, kk * 128:(kk + 1) * 128],
                                    lhsT=ln2tok[(NT - 1) % 2][:, k * 128:(k + 1) * 128],
                                    rhs=ident[:], start=True, stop=True)
            ins.then_inc(sem["t_lT"], 1)
        tensor.wait_ge(sem["s_lF"], 2 * NT)
        tensor.wait_ge(sem["v_ffnev"], NT - 1)
        for k in range(KC):
            tensor.matmul(out=psG[:, 0:512],
                          lhsT=ln2F[(NT - 1) % 2][:, k * 128:(k + 1) * 128],
                          rhs=wbig[:, k * D: k * D + 512],
                          start=(k == 0), stop=(k == KC - 1))
            ins = tensor.matmul(out=psG[:, 512:1024],
                                lhsT=ln2F[(NT - 1) % 2][:, k * 128:(k + 1) * 128],
                                rhs=wbig[:, k * D + 512: (k + 1) * D],
                                start=(k == 0), stop=(k == KC - 1))
        ins.then_inc(sem["t_ffn"], 1)

    @block.vector
    def _(vector):
        vector.memset(epsc[:], EPS)
        vector.wait_ge(sem["t_bc"], 1)
        vector.tensor_copy(out=bv_bc[:], in_=psv[0][:]).then_inc(sem["v_bc"], 1)
        vector.tensor_copy(out=bf_bc[:], in_=psv[1][:]).then_inc(sem["v_bc"], 1)
        def wv_cast(j):
            vector.wait_ge(*wsem(16 + j))
            vector.tensor_copy(out=wv_bf[:, j * D:(j + 1) * D], in_=wv_f32[j % 2][:]
                               ).then_inc(sem["v_cast"], 1)

        for t in range(NT):
            vector.wait_ge(*xsem(t))
            vector.bn_stats(out=bn12[:, 0:6], in_=x_t[t % 4][:, 0:512])
            vguard(vector, vector.bn_stats(out=bn12[:, 6:12], in_=x_t[t % 4][:, 512:1024]))
            vector.bn_aggr(out=mv_all[:, 2 * t:2 * t + 2], in_=bn12[:]
                           ).then_inc(sem["v_r0"], 1)
            if t < KC:
                wv_cast(t)
        vector.wait_ge(sem["s_inv0"], 1)
        vguard(vector, vector.tensor_tensor(out=norm_all[:], in0=sa16[:],
                                            in1=inv16[:], op=ALU.mult))
        vguard(vector, vector.tensor_scalar(out=vtmp[:], in0=norm_all[:],
                                            scalar1=NEPS, scalar2=None, op0=ALU.add))
        vguard(vector, vector.reciprocal(out=rn16[:], in_=vtmp[:]))
        vguard(vector, vector.tensor_tensor(out=k16[:], in0=inv16[:], in1=rn16[:],
                                            op=ALU.mult))
        for t in range(NT):
            vector.wait_ge(*xsemB(t))
            if t >= 1:
                vector.wait_ge(sem["t_scT"], 2 * t)
            vector.tensor_scalar(out=sctok[t % 2][:], in0=x_t[t % 4][:],
                                 scalar1=mv_all[:, 2 * t:2 * t + 1], scalar2=k16[:, t:t + 1],
                                 op0=ALU.subtract, op1=ALU.mult
                                 ).then_inc(sem["v_emit"], 1)
            if t >= 2:
                vector.wait_ge(sem["t_q"], t - 1)
                vector.tensor_tensor(out=vt[:, (t - 2) * D:(t - 1) * D], in0=psv[t % 2][:],
                                     in1=bv_bc[:], op=ALU.add).then_inc(sem["v_vt"], 1)
        for tc in range(NT - 2, NT):
            vector.wait_ge(sem["t_q"], tc + 1)
            vector.tensor_tensor(out=vt[:, tc * D:(tc + 1) * D], in0=psv[tc % 2][:],
                                 in1=bv_bc[:], op=ALU.add).then_inc(sem["v_vt"], 1)
        def wbuild_unit(u):
            m, chunk = u // 4, u % 4
            for c in range(4):
                q_ = m * 8 + chunk * 2 + c // 2
                cc = c % 2
                vector.wait_ge(*e1sem(q_))
                vector.wait_ge(sem["ld_e2"], 64 * (m + 1))
                e1src = e1_t[q_ % 2][:, cc * 512:(cc + 1) * 512]
                if c == 0:
                    vector.tensor_tensor(out=wacc[:], in0=e1src,
                                         in1=e2_sb[:, c * 512:(c + 1) * 512],
                                         op=ALU.mult).then_inc(sem["v_wbm"], 1)
                else:
                    vector.tensor_tensor(out=wtmp[:], in0=e1src,
                                         in1=e2_sb[:, c * 512:(c + 1) * 512],
                                         op=ALU.mult).then_inc(sem["v_wbm"], 1)
                    vector.tensor_tensor(out=wacc[:], in0=wacc[:], in1=wtmp[:],
                                         op=ALU.add)
            if m == 0:
                vector.tensor_scalar(out=wbig[:, chunk * D: chunk * D + 512],
                                     in0=wacc[:], scalar1=1.0, scalar2=None,
                                     op0=ALU.mult)
                vector.tensor_scalar(out=wbig[:, (chunk + 4) * D + 512: (chunk + 5) * D],
                                     in0=wacc[:], scalar1=1.0, scalar2=None,
                                     op0=ALU.mult).then_inc(sem["v_wb"], 1)
            else:
                vector.tensor_scalar(out=wbig[:, chunk * D + 512: (chunk + 1) * D],
                                     in0=wacc[:], scalar1=1.0, scalar2=None,
                                     op0=ALU.mult)
                vector.tensor_scalar(out=wbig[:, (chunk + 4) * D: (chunk + 4) * D + 512],
                                     in0=wacc[:], scalar1=-1.0, scalar2=None,
                                     op0=ALU.mult).then_inc(sem["v_wb"], 1)

        for s_ in range(16):
            vector.wait_ge(*wsem(s_))
            if s_ >= 2:
                vector.wait_ge(sem["t_q"], 16 + 4 * (s_ - 1))
            vector.tensor_copy(out=w_bf[s_ % 2][:], in_=w_f32[0][:]
                               ).then_inc(sem["v_cast"], 1)
            if 2 <= s_ <= 9:
                wbuild_unit(s_ - 2)
        vector.wait_ge(sem["v_scev"], 4)
        vector.reduce_max(out=negm_all[:, 0:1], in_=sc_sb[0][:],
                          axis=AX.X, negate=True).then_inc(sem["v_max"], 1)
        for i in range(NT):
            vector.wait_ge(sem["s_exp"], i + 1)
            vguard(vector, vector.reciprocal(out=rs_s[:], in_=sume_all[:, i:i + 1]))
            vguard(vector, vector.tensor_tensor(out=csc[:], in0=rs_s[:],
                                                in1=norm_all[:, i:i + 1], op=ALU.mult))
            if i < NT - 1:
                vector.wait_ge(sem["v_scev"], 4 * (i + 1) + 4)
                vector.reduce_max(out=negm_all[:, i + 1:i + 2], in_=sc_sb[(i + 1) % 2][:],
                                  axis=AX.X, negate=True).then_inc(sem["v_max"], 1)
            vector.wait_ge(sem["t_ctx"], i + 1)
            vector.tensor_scalar(out=attn[:], in0=psE[:], scalar1=csc[:], scalar2=None,
                                 op0=ALU.mult).then_inc(sem["v_ctxev"], 1)
            if i >= 1:
                vector.wait_ge(sem["t_ffn"], i)
                vector.tensor_tensor(out=ffn_f[:], in0=psG[:], in1=bf_bc[:],
                                     op=ALU.add).then_inc(sem["v_ffnev"], 1)
            vector.bn_stats(out=bn12[:, 0:6], in_=attn[:, 0:512])
            vguard(vector, vector.bn_stats(out=bn12[:, 6:12], in_=attn[:, 512:1024]))
            vector.bn_aggr(out=mvA[:], in_=bn12[:]).then_inc(sem["v_rA"], 1)
            vector.wait_ge(sem["s_invA"], i + 1)
            vector.tensor_scalar(out=ytmp[:], in0=attn[:], scalar1=mvA[:, 0:1],
                                 scalar2=invA[:], op0=ALU.subtract, op1=ALU.mult)
            vector.wait_ge(*xsem2(i))
            vector.tensor_tensor(out=y_buf[i % 3][:], in0=ytmp[:], in1=x_t[i % 4][:],
                                 op=ALU.add).then_inc(sem["v_y"], 1)
            vector.bn_stats(out=bn12[:, 0:6], in_=y_buf[i % 3][:, 0:512])
            vguard(vector, vector.bn_stats(out=bn12[:, 6:12], in_=y_buf[i % 3][:, 512:1024]))
            vector.bn_aggr(out=mvL[:], in_=bn12[:]).then_inc(sem["v_rL"], 1)
            vector.wait_ge(sem["s_invL"], i + 1)
            if i >= 2:
                vector.wait_ge(sem["t_lT"], 2 * (i - 2) + 2)
            vector.tensor_scalar(out=ln2tok[i % 2][:], in0=y_buf[i % 3][:],
                                 scalar1=mvL[:, 0:1], scalar2=invL[:],
                                 op0=ALU.subtract, op1=ALU.mult
                                 ).then_inc(sem["v_ln2e"], 1)
            if i >= 1:
                vector.bn_stats(out=bn12[:, 0:6], in_=ffn_f[:, 0:512])
                vguard(vector, vector.bn_stats(out=bn12[:, 6:12], in_=ffn_f[:, 512:1024]))
                vector.bn_aggr(out=mvF[:], in_=bn12[:]).then_inc(sem["v_rF"], 1)
                vector.wait_ge(sem["s_invF"], i)
                vector.tensor_scalar(out=ytmp[:], in0=ffn_f[:], scalar1=mvF[:, 0:1],
                                     scalar2=invF[:], op0=ALU.subtract, op1=ALU.mult)
                if i - 1 >= 2:
                    vector.wait_ge(*stsem(i - 3))
                vector.tensor_tensor(out=out_b[(i - 1) % 2][:], in0=ytmp[:],
                                     in1=y_buf[(i - 1) % 3][:], op=ALU.add
                                     ).then_inc(sem["v_out"], 1)
        vector.wait_ge(sem["t_ffn"], NT)
        vector.tensor_tensor(out=ffn_f[:], in0=psG[:], in1=bf_bc[:],
                             op=ALU.add).then_inc(sem["v_ffnev"], 1)
        vector.bn_stats(out=bn12[:, 0:6], in_=ffn_f[:, 0:512])
        vguard(vector, vector.bn_stats(out=bn12[:, 6:12], in_=ffn_f[:, 512:1024]))
        vector.bn_aggr(out=mvF[:], in_=bn12[:]).then_inc(sem["v_rF"], 1)
        vector.wait_ge(sem["s_invF"], NT)
        vector.tensor_scalar(out=ytmp[:], in0=ffn_f[:], scalar1=mvF[:, 0:1],
                             scalar2=invF[:], op0=ALU.subtract, op1=ALU.mult)
        vector.wait_ge(*stsem(NT - 3))
        vector.tensor_tensor(out=out_b[(NT - 1) % 2][:], in0=ytmp[:],
                             in1=y_buf[(NT - 1) % 3][:], op=ALU.add
                             ).then_inc(sem["v_out"], 1)

    @block.scalar
    def _(scalar):
        for t in range(1, NT, 2):
            if t >= 4:
                scalar.wait_ge(sem["v_r0"], t - 3)
            scalar.dma_start(out=x_t[t % 4][:], in_=x_hbm[t * 128:(t + 1) * 128, :]
                             ).then_inc(xsem(t)[0], 16)
        scalar.wait_ge(sem["t_bc"], 1)
        scalar.dma_start(out=w_f32[0][:], in_=wq_hbm[0]).then_inc(sem["ld_w0"], 16)
        scalar.wait_ge(sem["ld_c"], 16 * 5)
        scalar.wait_ge(sem["v_r0"], NT)
        var16 = mv_all[:].rearrange("p (t two) -> p two t", two=2)[:, 1, :]
        sguard(scalar, scalar.activation(out=lns16[:], in_=var16, func=AF.Ln, bias=epsc[:]))
        scalar.activation(out=inv16[:], in_=lns16[:], func=AF.Exp, scale=-0.5)
        sguard(scalar, scalar.activation(out=lns16[:], in_=var16, func=AF.Ln, scale=1024.0))
        scalar.activation(out=sa16[:], in_=lns16[:], func=AF.Exp, scale=0.5
                          ).then_inc(sem["s_inv0"], 1)
        for t in range(NT):
            scf4 = scf[:].rearrange("p (k t j) -> p k t j", k=KC, t=NT, j=128)
            scalar.wait_ge(sem["t_scT"], 2 * t + 1)
            scalar.copy(out=scf4[:, 0:4, t, :],
                        in_=psq[0][:].rearrange("p (k j) -> p k j", k=4)
                        ).then_inc(sem["s_scF"], 1)
            scalar.wait_ge(sem["t_scT"], 2 * t + 2)
            scalar.copy(out=scf4[:, 4:8, t, :],
                        in_=psq[1][:].rearrange("p (k j) -> p k j", k=4)
                        ).then_inc(sem["s_scF"], 1)
        scalar.wait_ge(sem["v_cast"], vcc(0))
        scalar.dma_start(out=w_f32[0][:], in_=wq_hbm[1]).then_inc(sem["ld_w1"], 16)
        for oc in range(KC):
            if oc + 2 < KC:
                scalar.wait_ge(sem["v_cast"], vcc(oc + 1))
                scalar.dma_start(out=w_f32[0][:], in_=wq_hbm[oc + 2]
                                 ).then_inc(wsem(oc + 2)[0], 16)
            elif oc + 2 < KC + 2:
                scalar.wait_ge(sem["v_cast"], vcc(oc + 1))
                scalar.dma_start(out=w_f32[0][:], in_=am_hbm[oc + 2 - KC]
                                 ).then_inc(wsem(oc + 2)[0], 16)
            for sl in range(4):
                scalar.wait_ge(sem["t_q"], 16 + 4 * oc + sl + 1)
                src_b = (psq[sl][:] if oc % 2 == 0
                         else psv[sl // 2][:, (sl % 2) * 512:(sl % 2) * 512 + 512])
                scalar.add(out=qf[:, oc * S + sl * 512: oc * S + (sl + 1) * 512],
                           in_=src_b, add=bq_t[:, oc:oc + 1]).then_inc(sem["s_qf"], 1)
        for k2 in range(KC):
            if k2 + 2 < KC:
                scalar.wait_ge(sem["v_cast"], vcc(9 + k2))
                scalar.dma_start(out=w_f32[0][:], in_=am_hbm[k2 + 2]
                                 ).then_inc(wsem(k2 + 10)[0], 16)
            for sl in range(4):
                scalar.wait_ge(sem["t_q"], 48 + 4 * k2 + sl + 1)
                src_b = (psq[sl][:] if k2 % 2 == 0
                         else psv[sl // 2][:, (sl % 2) * 512:(sl % 2) * 512 + 512])
                scalar.copy(out=tqf[:, k2 * S + sl * 512: k2 * S + (sl + 1) * 512],
                            in_=src_b).then_inc(sem["s_qf"], 1)
        sbank_s = [psA, psB, psC, psD]

        def sc_evac(j):
            for sl in range(4):
                g = 4 * j + sl
                scalar.wait_ge(sem["t_sc"], g + 1)
                scalar.mul(out=sc_sb[j % 2][:, sl * 512:(sl + 1) * 512],
                           in_=sbank_s[sl][:], mul=ISQ).then_inc(sem["v_scev"], 1)

        def s_exp(j):
            scalar.wait_ge(sem["v_max"], j + 1)
            if j >= 2:
                scalar.wait_ge(sem["t_pT"], 4 * (j - 2) + 4)
            scalar.activation(out=probs[j % 2][:], in_=sc_sb[j % 2][:], func=AF.Exp,
                              bias=negm_all[:, j:j + 1],
                              accum_out=sume_all[:, j:j + 1]).then_inc(sem["s_exp"], 1)

        def sc_evac_part(j, sls):
            for sl in sls:
                g = 4 * j + sl
                scalar.wait_ge(sem["t_sc"], g + 1)
                scalar.mul(out=sc_sb[j % 2][:, sl * 512:(sl + 1) * 512],
                           in_=sbank_s[sl][:], mul=ISQ).then_inc(sem["v_scev"], 1)

        def s_invL(j):
            scalar.wait_ge(sem["v_rL"], j + 1)
            sguard(scalar, scalar.activation(out=lns[:], in_=mvL[:, 1:2], func=AF.Ln, bias=epsc[:]))
            scalar.activation(out=invL[:], in_=lns[:], func=AF.Exp, scale=-0.5
                              ).then_inc(sem["s_invL"], 1)

        def s_invF(j):
            scalar.wait_ge(sem["v_rF"], j + 1)
            sguard(scalar, scalar.activation(out=lns[:], in_=mvF[:, 1:2], func=AF.Ln, bias=epsc[:]))
            scalar.activation(out=invF[:], in_=lns[:], func=AF.Exp, scale=-0.5
                              ).then_inc(sem["s_invF"], 1)

        def s_store(j):
            scalar.wait_ge(sem["v_out"], j + 1)
            if j >= 2:
                scalar.wait_ge(*stsem(j - 2))
            scalar.dma_start(out=y_hbm[j * 128: (j + 1) * 128, :],
                             in_=out_b[j % 2][:]).then_inc(stsem(j)[0], 16)

        sc_evac(0)
        s_exp(0)
        for i in range(NT):
            if i < NT - 1:
                scalar.wait_ge(sem["v_max"], i + 1)
                sc_evac_part(i + 1, (0, 1))
            if i >= 1:
                s_invL(i - 1)
            if i < NT - 1:
                sc_evac_part(i + 1, (2, 3))
            if i >= 1:
                scalar.wait_ge(sem["t_ctx"], i)
            for f in range(4):
                scalar.wait_ge(sem["t_pT"], 4 * i + f + 1)
                srcp = psE if f < 2 else psG
                scalar.copy(out=probsT[:, f * 512:(f + 1) * 512],
                            in_=srcp[:, (f % 2) * 512:(f % 2) * 512 + 512]
                            ).then_inc(sem["s_pT"], 1)
            if i >= 1:
                if i >= 2:
                    scalar.wait_ge(sem["t_ffn"], i - 1)
                for f in range(2):
                    scalar.wait_ge(sem["t_lT"], 2 * (i - 1) + f + 1)
                    scalar.copy(out=ln2F[(i - 1) % 2][:, f * 512:(f + 1) * 512],
                                in_=psAB[f][:]).then_inc(sem["s_lF"], 1)
            if i >= 2:
                s_invF(i - 2)
            if i < NT - 1:
                s_exp(i + 1)
            if i >= 2:
                s_store(i - 2)
            scalar.wait_ge(sem["v_rA"], i + 1)
            sguard(scalar, scalar.activation(out=lns[:], in_=mvA[:, 1:2], func=AF.Ln, bias=epsc[:]))
            scalar.activation(out=invA[:], in_=lns[:], func=AF.Exp, scale=-0.5
                              ).then_inc(sem["s_invA"], 1)
        s_invL(NT - 1)
        for f in range(2):
            scalar.wait_ge(sem["t_lT"], 2 * (NT - 1) + f + 1)
            scalar.copy(out=ln2F[(NT - 1) % 2][:, f * 512:(f + 1) * 512],
                        in_=psAB[f][:]).then_inc(sem["s_lF"], 1)
        s_invF(NT - 2)
        s_store(NT - 2)
        s_invF(NT - 1)
        s_store(NT - 1)
        scalar.wait_ge(sem["st0"], 16 * 8)
        scalar.wait_ge(sem["st1"], 16 * 8)

    if debug:
        @block.gpsimd
        def _(gpsimd):
            gpsimd.wait_ge(sem["st0"], 16 * 8)
            gpsimd.wait_ge(sem["st1"], 16 * 8)
            gpsimd.dma_start(out=dbg["d_qf"], in_=qf[:]).then_inc(sem["dbg"], 16)
            gpsimd.dma_start(out=dbg["d_tqf"], in_=tqf[:]).then_inc(sem["dbg"], 16)
            gpsimd.dma_start(out=dbg["d_vt"], in_=vt[:]).then_inc(sem["dbg"], 16)
            gpsimd.dma_start(out=dbg["d_wbig"], in_=wbig[:]).then_inc(sem["dbg"], 16)
            gpsimd.dma_start(out=dbg["d_norm"], in_=norm_all[:]).then_inc(sem["dbg"], 16)
            gpsimd.dma_start(out=dbg["d_sums"], in_=sume_all[:]).then_inc(sem["dbg"], 16)
            gpsimd.wait_ge(sem["dbg"], 16 * 6)

    es.close()
    return nc


def _tile8(w):
    return np.ascontiguousarray(
        np.transpose(w.reshape(8, 128, 8, 128), (2, 1, 0, 3)).reshape(8, 128, 1024))


def stage_inputs(L0_attn_hidden_states, L0_attn_Wq, L0_attn_bq, L0_attn_Wv, L0_attn_bv,
                 L0_attn_anomaly_matrix, L0_ffn_G1_R, L0_ffn_G2_R, L0_ffn_G1_I,
                 L0_ffn_G2_I, L0_ffn_bias_real, L0_ffn_bias_imag):
    f32 = np.float32
    x = np.asarray(L0_attn_hidden_states, dtype=f32)
    Wq = np.asarray(L0_attn_Wq, dtype=f32)
    Wv = np.asarray(L0_attn_Wv, dtype=f32)
    A = np.asarray(L0_attn_anomaly_matrix, dtype=f32)
    bq = np.asarray(L0_attn_bq, dtype=f32)
    bv = np.asarray(L0_attn_bv, dtype=f32)
    G1R = np.asarray(L0_ffn_G1_R, dtype=f32)
    G2R = np.asarray(L0_ffn_G2_R, dtype=f32)
    G1I = np.asarray(L0_ffn_G1_I, dtype=f32)
    G2I = np.asarray(L0_ffn_G2_I, dtype=f32)
    br = np.asarray(L0_ffn_bias_real, dtype=f32)
    bi = np.asarray(L0_ffn_bias_imag, dtype=f32)

    common = dict(
        wq_t=_tile8(np.ascontiguousarray(Wq.T)),
        amat_t=_tile8(A),
        wv_t=np.ascontiguousarray(Wv.T).reshape(8, 128, 1024),
        bq_t=np.ascontiguousarray(bq.reshape(8, 128).T),
        bv_row=bv.reshape(1, 1024),
        brbi_row=np.concatenate([br, bi]).reshape(1, 1024),
        ones_row=np.ones((1, 128), dtype=f32),
        ident=np.eye(128, dtype=ml_dtypes.bfloat16),
        e1=np.ascontiguousarray(
            np.stack([
                np.stack([np.repeat(np.repeat(np.ascontiguousarray(G1[0, :, c, :].T), 16, 0),
                                    16, 1) for c in range(4)])
                for G1 in (G1R, G1I)]).astype(f32)
            .reshape(2, 2, 2, 4, 128, 512)
            .transpose(0, 3, 1, 4, 2, 5)
            .reshape(2, 4, 2, 128, 1024)),
        e2=np.stack([
            np.stack([np.tile(np.ascontiguousarray(G2[c, :, 0, :].T), (8, 32))
                      for c in range(4)])
            for G2 in (G2R, G2I)]).astype(f32),
    )
    in_maps = [dict(common, x=np.ascontiguousarray(x[b])) for b in range(8)]
    return in_maps


def kernel(**inputs):
    in_maps = stage_inputs(**inputs)
    nc = build_nc()
    res = run_bass_kernel_spmd(nc, in_maps, core_ids=list(range(8)))
    return np.stack([res.results[c]["y_out"] for c in range(8)], axis=0)


# version 18 (final)
# speedup vs baseline: 1.0515x; 1.0209x over previous
import numpy as np
import ml_dtypes
from contextlib import ExitStack
import concourse.bass as bass
import concourse.mybir as mybir
from concourse.bass_utils import run_bass_kernel_spmd

F32 = mybir.dt.float32
BF16 = mybir.dt.bfloat16
AF = mybir.ActivationFunctionType
ALU = mybir.AluOpType
AX = mybir.AxisListType

D = 1024
S = 2048
NT = 16
KC = 8
EPS = 1e-5
NEPS = 1e-9
ISQ = 1.0 / 32.0


def build_nc(debug=False):
    nc = bass.Bass("TRN2", target_bir_lowering=False, detect_race_conditions=False)

    x_hbm = nc.dram_tensor("x", [S, D], F32, kind="ExternalInput").ap()
    wq_hbm = nc.dram_tensor("wq_t", [KC, 128, D], F32, kind="ExternalInput").ap()
    am_hbm = nc.dram_tensor("amat_t", [KC, 128, D], F32, kind="ExternalInput").ap()
    wv_hbm = nc.dram_tensor("wv_t", [KC, 128, D], F32, kind="ExternalInput").ap()
    bq_hbm = nc.dram_tensor("bq_t", [128, KC], F32, kind="ExternalInput").ap()
    bv_hbm = nc.dram_tensor("bv_row", [1, D], F32, kind="ExternalInput").ap()
    bf_hbm = nc.dram_tensor("brbi_row", [1, D], F32, kind="ExternalInput").ap()
    on_hbm = nc.dram_tensor("ones_row", [1, 128], F32, kind="ExternalInput").ap()
    id_hbm = nc.dram_tensor("ident", [128, 128], BF16, kind="ExternalInput").ap()
    e1_hbm = nc.dram_tensor("e1", [2, 4, 2, 128, 1024], F32, kind="ExternalInput").ap()
    e2_hbm = nc.dram_tensor("e2", [2, 4, 128, 512], F32, kind="ExternalInput").ap()

    y_hbm = nc.dram_tensor("y_out", [S, D], F32, kind="ExternalOutput").ap()
    if debug:
        dbg = {
            "d_qf": nc.dram_tensor("d_qf", [128, KC * S], BF16, kind="ExternalOutput").ap(),
            "d_tqf": nc.dram_tensor("d_tqf", [128, KC * S], BF16, kind="ExternalOutput").ap(),
            "d_vt": nc.dram_tensor("d_vt", [128, NT * D], BF16, kind="ExternalOutput").ap(),
            "d_wbig": nc.dram_tensor("d_wbig", [128, KC * D], BF16, kind="ExternalOutput").ap(),
            "d_norm": nc.dram_tensor("d_norm", [128, NT], F32, kind="ExternalOutput").ap(),
            "d_sums": nc.dram_tensor("d_sums", [128, NT], F32, kind="ExternalOutput").ap(),
        }

    es = ExitStack()

    qf = es.enter_context(nc.sbuf_tensor("qf", [128, KC * S], BF16))
    tqf = es.enter_context(nc.sbuf_tensor("tqf", [128, KC * S], BF16))
    vt = es.enter_context(nc.sbuf_tensor("vt", [128, NT * D], BF16))
    wbig = es.enter_context(nc.sbuf_tensor("wbig", [128, KC * D], BF16))
    x_t = [es.enter_context(nc.sbuf_tensor(f"x_t{i}", [128, D], F32)) for i in range(4)]
    bq_t = es.enter_context(nc.sbuf_tensor("sb_bq", [128, KC], F32))
    ident = es.enter_context(nc.sbuf_tensor("sb_id", [128, 128], BF16))
    bf_bc = es.enter_context(nc.sbuf_tensor("bf_bc", [128, D], F32))
    ones_bf = es.enter_context(nc.sbuf_tensor("ones_bf", [1, 128], BF16))
    bv_bfr = es.enter_context(nc.sbuf_tensor("bv_bfr", [1, D], BF16))
    nm16 = es.enter_context(nc.sbuf_tensor("nm16", [128, NT], F32))
    norm_all = es.enter_context(nc.sbuf_tensor("norm_all", [128, NT], F32))
    negm_all = es.enter_context(nc.sbuf_tensor("negm_all", [128, NT], F32))
    sume_all = es.enter_context(nc.sbuf_tensor("sume_all", [128, NT], F32))
    bn12 = es.enter_context(nc.sbuf_tensor("bn12", [128, 12], F32))
    mv_all = es.enter_context(nc.sbuf_tensor("mv_all", [128, 2 * NT], F32))
    inv16 = es.enter_context(nc.sbuf_tensor("inv16", [128, NT], F32))
    sa16 = es.enter_context(nc.sbuf_tensor("sa16", [128, NT], F32))
    k16 = es.enter_context(nc.sbuf_tensor("k16", [128, NT], F32))
    vtmp = es.enter_context(nc.sbuf_tensor("vtmp", [128, NT], F32))
    epsc = es.enter_context(nc.sbuf_tensor("epsc", [128, 1], F32))
    rn16 = es.enter_context(nc.sbuf_tensor("rn16", [128, NT], F32))
    lns16 = es.enter_context(nc.sbuf_tensor("lns16", [128, NT], F32))
    lns = es.enter_context(nc.sbuf_tensor("lns", [128, 1], F32))

    es1 = ExitStack()
    scf = es1.enter_context(nc.sbuf_tensor("scf", [128, KC * S], BF16))
    sctok = [es1.enter_context(nc.sbuf_tensor(f"sctok{i}", [128, D], BF16)) for i in range(2)]
    es1r = ExitStack()
    bv_row = es1r.enter_context(nc.sbuf_tensor("sb_bvr", [1, D], F32))
    bf_row = es1r.enter_context(nc.sbuf_tensor("bf_row", [1, D], F32))
    ones_r = es1r.enter_context(nc.sbuf_tensor("ones_r", [1, 128], F32))
    es1r.close()
    wv_f32 = [es1.enter_context(nc.sbuf_tensor(f"wv_f32_{i}", [128, D], F32)) for i in range(2)]
    es1a = ExitStack()
    e1_t = [es1a.enter_context(nc.sbuf_tensor(f"e1_t{i}", [128, 1024], F32)) for i in range(2)]
    e2_sb = es1a.enter_context(nc.sbuf_tensor("e2_sb", [128, 2048], F32))
    wacc = es1a.enter_context(nc.sbuf_tensor("wacc", [128, 512], F32))
    wtmp = es1a.enter_context(nc.sbuf_tensor("wtmp", [128, 512], F32))
    w_f32 = [es1a.enter_context(nc.sbuf_tensor("w_f32_0", [128, D], F32))] * 2
    w_bf = [es1a.enter_context(nc.sbuf_tensor(f"w_bf_{i}", [128, D], BF16)) for i in range(2)]
    es1a.close()
    es1b = ExitStack()
    wv_bf = es1b.enter_context(nc.sbuf_tensor("wv_bf", [128, KC * D], BF16))
    es1b.close()
    es1.close()

    es2 = ExitStack()
    sc_sb = [es2.enter_context(nc.sbuf_tensor(f"sc_sb{i}", [128, S], F32)) for i in range(2)]
    probs = [es2.enter_context(nc.sbuf_tensor(f"probs{i}", [128, S], BF16)) for i in range(2)]
    probsT = es2.enter_context(nc.sbuf_tensor("probsT", [128, S], BF16))
    attn = es2.enter_context(nc.sbuf_tensor("attn", [128, D], F32))
    y_buf = [es2.enter_context(nc.sbuf_tensor(f"y_buf{i}", [128, D], F32)) for i in range(3)]
    ln2tok = [es2.enter_context(nc.sbuf_tensor(f"ln2tok{i}", [128, D], BF16)) for i in range(2)]
    ln2F = [es2.enter_context(nc.sbuf_tensor(f"ln2F{i}", [128, D], BF16)) for i in range(2)]
    ffn_f = es2.enter_context(nc.sbuf_tensor("ffn_f", [128, D], F32))
    out_b = [es2.enter_context(nc.sbuf_tensor(f"out_b{i}", [128, D], F32)) for i in range(2)]
    mvA = es2.enter_context(nc.sbuf_tensor("mvA", [128, 2], F32))
    mvL = es2.enter_context(nc.sbuf_tensor("mvL", [128, 2], F32))
    mvF = es2.enter_context(nc.sbuf_tensor("mvF", [128, 2], F32))
    invA = es2.enter_context(nc.sbuf_tensor("invA", [128, 1], F32))
    invL = es2.enter_context(nc.sbuf_tensor("invL", [128, 1], F32))
    invF = es2.enter_context(nc.sbuf_tensor("invF", [128, 1], F32))
    rs_s = es2.enter_context(nc.sbuf_tensor("rs_s", [128, 1], F32))
    csc = es2.enter_context(nc.sbuf_tensor("csc", [128, 1], F32))
    ytmp = es2.enter_context(nc.sbuf_tensor("ytmp", [128, D], F32))
    es2.close()

    esp1 = ExitStack()
    psq = [esp1.enter_context(nc.psum_tensor(f"psq{i}", [128, 512], F32)) for i in range(4)]
    psv = [esp1.enter_context(nc.psum_tensor(f"psv{i}", [128, 1024], F32)) for i in range(2)]
    esp1.close()
    esp2 = ExitStack()
    psA = esp2.enter_context(nc.psum_tensor("psA", [128, 512], F32))
    psB = esp2.enter_context(nc.psum_tensor("psB", [128, 512], F32))
    psC = esp2.enter_context(nc.psum_tensor("psC", [128, 512], F32))
    psD = esp2.enter_context(nc.psum_tensor("psD", [128, 512], F32))
    psE = esp2.enter_context(nc.psum_tensor("psE", [128, 1024], F32))
    psG = esp2.enter_context(nc.psum_tensor("psG", [128, 1024], F32))
    esp2.close()
    psAB = [psA, psB]
    psCD = [psC, psD]

    semnames = ("ld_c ld_x0 ld_x1 ld_x2 ld_x3 ld_e1a ld_e1b ld_e2 ld_w0 ld_w1 ld_wv st0 st1 t_bc v_bc v_r0 s_inv0 v_k s_emit s_vt v_emit t_scT s_scF "
                "v_cast t_q s_qf v_wbm v_wb dbg v_g s_g t_sc v_scev v_max s_exp t_pT s_pT "
                "t_lT s_lF v_ln2e t_ctx v_ctxev t_ffn v_ffnev v_y v_out v_rA s_invA v_rL "
                "s_invL v_rF s_invF").split()
    sem = {n: es.enter_context(nc.semaphore(n)) for n in semnames}
    block = es.enter_context(nc.Block())

    XB = [4, 4, 4, 4]

    _vg = [0]
    _sg = [0]

    def vguard(eng, ins):
        _vg[0] += 1
        ins.then_inc(sem["v_g"], 1)
        eng.wait_ge(sem["v_g"], _vg[0])

    def sguard(eng, ins):
        _sg[0] += 1
        ins.then_inc(sem["s_g"], 1)
        eng.wait_ge(sem["s_g"], _sg[0])

    def xsem(t):
        return sem[f"ld_x{t % 4}"], 16 * (t // 4 + 1)

    def xsem2(i):
        return sem[f"ld_x{i % 4}"], 16 * (XB[i % 4] + i // 4 + 1)

    def emit_gate(t):
        if t % 2 == 0:
            return sem["v_emit"], t // 2 + 1
        return sem["s_emit"], (t + 1) // 2

    def wsem(s_):
        if s_ >= 16:
            return sem["ld_wv"], 16 * (s_ - 16 + 1)
        return sem[f"ld_w{s_ % 2}"], 16 * (s_ // 2 + 1)

    def vcc(stage):
        return stage + 9

    def e1sem(q_):
        return sem["ld_e1a" if q_ % 2 == 0 else "ld_e1b"], 16 * (q_ // 2 + 1)

    def stsem(j):
        return sem[f"st{j % 2}"], 16 * (j // 2 + 1)


    @block.sync
    def _(sync):
        sync.dma_start(out=ones_r[:], in_=on_hbm).then_inc(sem["ld_c"], 16)
        sync.dma_start(out=bv_row[:], in_=bv_hbm).then_inc(sem["ld_c"], 16)
        sync.dma_start(out=bf_row[:], in_=bf_hbm).then_inc(sem["ld_c"], 16)
        sync.dma_start(out=ident[:], in_=id_hbm).then_inc(sem["ld_c"], 16)
        sync.dma_start(out=bq_t[:], in_=bq_hbm).then_inc(sem["ld_c"], 16)
        for t in range(NT):
            if 1 <= t <= KC:
                j = t - 1
                if j <= 1:
                    sync.wait_ge(sem["t_bc"], 1)
                    sync.wait_ge(sem["v_bc"], 2)
                else:
                    sync.wait_ge(sem["v_cast"], j - 1)
                sync.dma_start(out=wv_f32[j % 2][:], in_=wv_hbm[j]).then_inc(wsem(16 + j)[0], 16)
            if t % 2 == 0:
                if t >= 4:
                    sync.wait_ge(sem["v_emit"], (t - 4) // 2 + 1)
                sync.dma_start(out=x_t[t % 4][:], in_=x_hbm[t * 128:(t + 1) * 128, :]
                               ).then_inc(xsem(t)[0], 16)
        sync.wait_ge(sem["t_bc"], 1)
        sync.wait_ge(sem["t_q"], 16)
        for m in range(2):
            if m == 1:
                sync.wait_ge(sem["v_wbm"], 16)
            for c in range(4):
                sync.dma_start(out=e2_sb[:, c * 512:(c + 1) * 512],
                               in_=e2_hbm[m, c]).then_inc(sem["ld_e2"], 16)
            for chunk in range(4):
                for pair in range(2):
                    q_ = m * 8 + chunk * 2 + pair
                    if q_ >= 2:
                        sync.wait_ge(sem["v_wbm"], 2 * q_ - 2)
                    sync.dma_start(out=e1_t[q_ % 2][:],
                                   in_=e1_hbm[m, chunk, pair]
                                   ).then_inc(e1sem(q_)[0], 16)
        for i in range(NT):
            if i <= 3:
                sync.wait_ge(*emit_gate(i + 12))
            else:
                sync.wait_ge(sem["v_y"], i - 3)
            sync.dma_start(out=x_t[i % 4][:], in_=x_hbm[i * 128:(i + 1) * 128, :]
                           ).then_inc(xsem2(i)[0], 16)

    @block.tensor
    def _(tensor):
        tensor.wait_ge(sem["ld_c"], 16 * 3)
        tensor.matmul(out=psv[1][:, 0:512], lhsT=ones_r[:], rhs=bf_row[:, 0:512],
                      start=True, stop=True)
        tensor.matmul(out=psv[1][:, 512:1024], lhsT=ones_r[:], rhs=bf_row[:, 512:1024],
                      start=True, stop=True).then_inc(sem["t_bc"], 1)

        def v_group(tc):
            tensor.wait_ge(sem["s_scF"], 2 * tc + 2)
            if tc == 0:
                tensor.wait_ge(sem["v_bc"], 3)
            if tc >= 2:
                tensor.wait_ge(sem["s_vt"], tc - 1)
            for half in range(2):
                tensor.matmul(out=psv[tc % 2][:, half * 512:(half + 1) * 512],
                              lhsT=ones_bf[:], rhs=bv_bfr[:, half * 512:(half + 1) * 512],
                              start=True, stop=False)
                for k in range(KC):
                    if tc == 0:
                        tensor.wait_ge(sem["v_cast"], k + 1)
                    ins = tensor.matmul(out=psv[tc % 2][:, half * 512:(half + 1) * 512],
                                        lhsT=scf[:, k * S + tc * 128: k * S + (tc + 1) * 128],
                                        rhs=wv_bf[:, k * D + half * 512: k * D + half * 512 + 512],
                                        start=False, stop=(k == KC - 1))
            ins.then_inc(sem["t_q"], 1)

        tensor.wait_ge(sem["ld_c"], 16 * 4)
        for t in range(NT):
            tensor.wait_ge(*emit_gate(t))
            if t >= 1:
                tensor.wait_ge(sem["s_scF"], 2 * t - 1)
            for k in range(4):
                ins = tensor.matmul(out=psq[0][:, k * 128:(k + 1) * 128],
                                    lhsT=sctok[t % 2][:, k * 128:(k + 1) * 128],
                                    rhs=ident[:], start=True, stop=True)
            ins.then_inc(sem["t_scT"], 1)
            if t >= 1:
                tensor.wait_ge(sem["s_scF"], 2 * t)
            for k in range(4, 8):
                ins = tensor.matmul(out=psq[1][:, (k - 4) * 128:(k - 3) * 128],
                                    lhsT=sctok[t % 2][:, k * 128:(k + 1) * 128],
                                    rhs=ident[:], start=True, stop=True)
            ins.then_inc(sem["t_scT"], 1)
            if t >= 4:
                v_group(t - 4)
        for tc in range(NT - 4, NT):
            v_group(tc)
        def qbank(oc, sl):
            if oc % 2 == 0:
                return psq[sl][:]
            return psv[sl // 2][:, (sl % 2) * 512:(sl % 2) * 512 + 512]

        tensor.wait_ge(sem["s_vt"], NT)
        for oc in range(KC):
            tensor.wait_ge(sem["v_cast"], vcc(oc))
            if oc >= 2:
                tensor.wait_ge(sem["s_qf"], 4 * oc - 4)
            for k in range(KC):
                for sl in range(4):
                    ins = tensor.matmul(out=qbank(oc, sl),
                                        lhsT=w_bf[oc % 2][:, k * 128:(k + 1) * 128],
                                        rhs=scf[:, k * S + sl * 512: k * S + (sl + 1) * 512],
                                        start=(k == 0), stop=(k == KC - 1))
                    if k == KC - 1:
                        ins.then_inc(sem["t_q"], 1)
        for k2 in range(KC):
            tensor.wait_ge(sem["v_cast"], vcc(8 + k2))
            tensor.wait_ge(sem["s_qf"], max(32, 28 + 4 * k2))
            for h in range(KC):
                for sl in range(4):
                    ins = tensor.matmul(out=qbank(k2, sl),
                                        lhsT=w_bf[k2 % 2][:, h * 128:(h + 1) * 128],
                                        rhs=qf[:, h * S + sl * 512: h * S + (sl + 1) * 512],
                                        start=(h == 0), stop=(h == KC - 1))
                    if h == KC - 1:
                        ins.then_inc(sem["t_q"], 1)

        tensor.wait_ge(sem["s_qf"], 64)
        tensor.wait_ge(sem["s_vt"], NT)
        tensor.wait_ge(sem["v_wb"], 8)

        sbank = [psA, psB, psC, psD]

        def scores_mms(j):
            for sl in range(4):
                if j >= 1:
                    tensor.wait_ge(sem["v_scev"], 4 * (j - 1) + sl + 1)
                if sl < 2 and j >= 3:
                    tensor.wait_ge(sem["s_lF"], 2 * (j - 3) + sl + 1)
                for k in range(KC):
                    ins = tensor.matmul(
                        out=sbank[sl][:],
                        lhsT=qf[:, k * S + j * 128: k * S + (j + 1) * 128],
                        rhs=tqf[:, k * S + sl * 512: k * S + (sl + 1) * 512],
                        start=(k == 0), stop=(k == KC - 1))
                ins.then_inc(sem["t_sc"], 1)

        scores_mms(0)
        for i in range(NT):
            if i < NT - 1:
                scores_mms(i + 1)
            tensor.wait_ge(sem["s_exp"], i + 1)
            if i >= 1:
                tensor.wait_ge(sem["v_ctxev"], i)
            if i >= 2:
                tensor.wait_ge(sem["v_ffnev"], i - 1)
            for f in range(4):
                dst = psE if f < 2 else psG
                half = (f % 2) * 512
                for jj in range(4):
                    j = f * 4 + jj
                    ins = tensor.matmul(out=dst[:, half + jj * 128: half + (jj + 1) * 128],
                                        lhsT=probs[i % 2][:, j * 128:(j + 1) * 128],
                                        rhs=ident[:], start=True, stop=True)
                ins.then_inc(sem["t_pT"], 1)
            if i >= 1:
                tensor.wait_ge(sem["v_ln2e"], i)
                for f in range(2):
                    if i < NT - 1:
                        tensor.wait_ge(sem["v_scev"], 4 * (i + 1) + f + 1)
                    else:
                        tensor.wait_ge(sem["s_lF"], 2 * (i - 2) + f + 1)
                    for kk in range(4):
                        k = f * 4 + kk
                        ins = tensor.matmul(out=psAB[f][:, kk * 128:(kk + 1) * 128],
                                            lhsT=ln2tok[(i - 1) % 2][:, k * 128:(k + 1) * 128],
                                            rhs=ident[:], start=True, stop=True)
                    ins.then_inc(sem["t_lT"], 1)
            tensor.wait_ge(sem["s_pT"], 4 * i + 4)
            for j in range(NT):
                if False:
                    pass
                tensor.matmul(out=psE[:, 0:512],
                              lhsT=probsT[:, j * 128:(j + 1) * 128],
                              rhs=vt[:, j * D: j * D + 512],
                              start=(j == 0), stop=(j == NT - 1))
                ins = tensor.matmul(out=psE[:, 512:1024],
                                    lhsT=probsT[:, j * 128:(j + 1) * 128],
                                    rhs=vt[:, j * D + 512: (j + 1) * D],
                                    start=(j == 0), stop=(j == NT - 1))
            ins.then_inc(sem["t_ctx"], 1)
            if i >= 1:
                tensor.wait_ge(sem["s_lF"], 2 * i)
                tensor.wait_ge(sem["v_ffnev"], i - 1)
                for k in range(KC):
                    tensor.matmul(out=psG[:, 0:512],
                                  lhsT=ln2F[(i - 1) % 2][:, k * 128:(k + 1) * 128],
                                  rhs=wbig[:, k * D: k * D + 512],
                                  start=(k == 0), stop=(k == KC - 1))
                    ins = tensor.matmul(out=psG[:, 512:1024],
                                        lhsT=ln2F[(i - 1) % 2][:, k * 128:(k + 1) * 128],
                                        rhs=wbig[:, k * D + 512: (k + 1) * D],
                                        start=(k == 0), stop=(k == KC - 1))
                ins.then_inc(sem["t_ffn"], 1)
        i = NT
        tensor.wait_ge(sem["v_ln2e"], NT)
        for f in range(2):
            tensor.wait_ge(sem["s_lF"], 2 * (NT - 2) + f + 1)
            for kk in range(4):
                k = f * 4 + kk
                ins = tensor.matmul(out=psAB[f][:, kk * 128:(kk + 1) * 128],
                                    lhsT=ln2tok[(NT - 1) % 2][:, k * 128:(k + 1) * 128],
                                    rhs=ident[:], start=True, stop=True)
            ins.then_inc(sem["t_lT"], 1)
        tensor.wait_ge(sem["s_lF"], 2 * NT)
        tensor.wait_ge(sem["v_ffnev"], NT - 1)
        for k in range(KC):
            tensor.matmul(out=psG[:, 0:512],
                          lhsT=ln2F[(NT - 1) % 2][:, k * 128:(k + 1) * 128],
                          rhs=wbig[:, k * D: k * D + 512],
                          start=(k == 0), stop=(k == KC - 1))
            ins = tensor.matmul(out=psG[:, 512:1024],
                                lhsT=ln2F[(NT - 1) % 2][:, k * 128:(k + 1) * 128],
                                rhs=wbig[:, k * D + 512: (k + 1) * D],
                                start=(k == 0), stop=(k == KC - 1))
        ins.then_inc(sem["t_ffn"], 1)

    @block.vector
    def _(vector):
        vector.memset(epsc[:], EPS)
        vector.wait_ge(sem["ld_c"], 16 * 2)
        vector.tensor_copy(out=ones_bf[:], in_=ones_r[:]).then_inc(sem["v_bc"], 1)
        vector.tensor_copy(out=bv_bfr[:], in_=bv_row[:]).then_inc(sem["v_bc"], 1)

        def wv_cast(j):
            vector.wait_ge(*wsem(16 + j))
            vector.tensor_copy(out=wv_bf[:, j * D:(j + 1) * D], in_=wv_f32[j % 2][:]
                               ).then_inc(sem["v_cast"], 1)

        mean16 = mv_all[:].rearrange("p (t two) -> p two t", two=2)[:, 0, :]

        def chain_v(g):
            c2 = slice(2 * g, 2 * g + 2)
            vector.wait_ge(sem["s_inv0"], g + 1)
            vguard(vector, vector.tensor_tensor(out=norm_all[:, c2], in0=sa16[:, c2],
                                                in1=inv16[:, c2], op=ALU.mult))
            vguard(vector, vector.tensor_scalar(out=vtmp[:, c2], in0=norm_all[:, c2],
                                                scalar1=NEPS, scalar2=None, op0=ALU.add))
            vguard(vector, vector.reciprocal(out=rn16[:, c2], in_=vtmp[:, c2]))
            vguard(vector, vector.tensor_tensor(out=k16[:, c2], in0=inv16[:, c2],
                                                in1=rn16[:, c2], op=ALU.mult))
            vector.tensor_scalar(out=nm16[:, c2], in0=mean16[:, c2], scalar1=-1.0,
                                 scalar2=None, op0=ALU.mult).then_inc(sem["v_k"], 1)

        def emit_v(t):
            if t >= 2:
                vector.wait_ge(sem["t_scT"], 2 * t - 2)
            vector.tensor_scalar(out=sctok[t % 2][:], in0=x_t[t % 4][:],
                                 scalar1=mv_all[:, 2 * t:2 * t + 1], scalar2=k16[:, t:t + 1],
                                 op0=ALU.subtract, op1=ALU.mult
                                 ).then_inc(sem["v_emit"], 1)

        for t in range(NT):
            if t >= 2 and t % 2 == 0:
                chain_v((t - 2) // 2)
                emit_v(t - 2)
            vector.wait_ge(*xsem(t))
            vector.bn_stats(out=bn12[:, 0:6], in_=x_t[t % 4][:, 0:512])
            vguard(vector, vector.bn_stats(out=bn12[:, 6:12], in_=x_t[t % 4][:, 512:1024]))
            vector.bn_aggr(out=mv_all[:, 2 * t:2 * t + 2], in_=bn12[:]
                           ).then_inc(sem["v_r0"], 1)
            if t == 0:
                vector.wait_ge(sem["t_bc"], 1)
                vector.tensor_copy(out=bf_bc[:], in_=psv[1][:]).then_inc(sem["v_bc"], 1)
            if t < KC:
                wv_cast(t)
        chain_v(7)
        emit_v(14)
        def wbuild_unit(u):
            m, chunk = u // 4, u % 4
            for c in range(4):
                q_ = m * 8 + chunk * 2 + c // 2
                cc = c % 2
                vector.wait_ge(*e1sem(q_))
                vector.wait_ge(sem["ld_e2"], 64 * (m + 1))
                e1src = e1_t[q_ % 2][:, cc * 512:(cc + 1) * 512]
                if c == 0:
                    vector.tensor_tensor(out=wacc[:], in0=e1src,
                                         in1=e2_sb[:, c * 512:(c + 1) * 512],
                                         op=ALU.mult).then_inc(sem["v_wbm"], 1)
                else:
                    vector.tensor_tensor(out=wtmp[:], in0=e1src,
                                         in1=e2_sb[:, c * 512:(c + 1) * 512],
                                         op=ALU.mult).then_inc(sem["v_wbm"], 1)
                    vector.tensor_tensor(out=wacc[:], in0=wacc[:], in1=wtmp[:],
                                         op=ALU.add)
            if m == 0:
                vector.tensor_scalar(out=wbig[:, chunk * D: chunk * D + 512],
                                     in0=wacc[:], scalar1=1.0, scalar2=None,
                                     op0=ALU.mult)
                vector.tensor_scalar(out=wbig[:, (chunk + 4) * D + 512: (chunk + 5) * D],
                                     in0=wacc[:], scalar1=1.0, scalar2=None,
                                     op0=ALU.mult).then_inc(sem["v_wb"], 1)
            else:
                vector.tensor_scalar(out=wbig[:, chunk * D + 512: (chunk + 1) * D],
                                     in0=wacc[:], scalar1=1.0, scalar2=None,
                                     op0=ALU.mult)
                vector.tensor_scalar(out=wbig[:, (chunk + 4) * D: (chunk + 4) * D + 512],
                                     in0=wacc[:], scalar1=-1.0, scalar2=None,
                                     op0=ALU.mult).then_inc(sem["v_wb"], 1)

        for s_ in range(16):
            vector.wait_ge(*wsem(s_))
            if s_ >= 2:
                vector.wait_ge(sem["t_q"], 16 + 4 * (s_ - 1))
            vector.tensor_copy(out=w_bf[s_ % 2][:], in_=w_f32[0][:]
                               ).then_inc(sem["v_cast"], 1)
            if 2 <= s_ <= 9:
                wbuild_unit(s_ - 2)
        vector.wait_ge(sem["v_scev"], 4)
        vector.reduce_max(out=negm_all[:, 0:1], in_=sc_sb[0][:],
                          axis=AX.X, negate=True).then_inc(sem["v_max"], 1)
        for i in range(NT):
            vector.wait_ge(sem["s_exp"], i + 1)
            vguard(vector, vector.reciprocal(out=rs_s[:], in_=sume_all[:, i:i + 1]))
            vguard(vector, vector.tensor_tensor(out=csc[:], in0=rs_s[:],
                                                in1=norm_all[:, i:i + 1], op=ALU.mult))
            if i < NT - 1:
                vector.wait_ge(sem["v_scev"], 4 * (i + 1) + 4)
                vector.reduce_max(out=negm_all[:, i + 1:i + 2], in_=sc_sb[(i + 1) % 2][:],
                                  axis=AX.X, negate=True).then_inc(sem["v_max"], 1)
            vector.wait_ge(sem["t_ctx"], i + 1)
            vector.tensor_scalar(out=attn[:], in0=psE[:], scalar1=csc[:], scalar2=None,
                                 op0=ALU.mult).then_inc(sem["v_ctxev"], 1)
            if i >= 1:
                vector.wait_ge(sem["t_ffn"], i)
                vector.tensor_tensor(out=ffn_f[:], in0=psG[:], in1=bf_bc[:],
                                     op=ALU.add).then_inc(sem["v_ffnev"], 1)
            vector.bn_stats(out=bn12[:, 0:6], in_=attn[:, 0:512])
            vguard(vector, vector.bn_stats(out=bn12[:, 6:12], in_=attn[:, 512:1024]))
            vector.bn_aggr(out=mvA[:], in_=bn12[:]).then_inc(sem["v_rA"], 1)
            vector.wait_ge(sem["s_invA"], i + 1)
            vector.tensor_scalar(out=ytmp[:], in0=attn[:], scalar1=mvA[:, 0:1],
                                 scalar2=invA[:], op0=ALU.subtract, op1=ALU.mult)
            vector.wait_ge(*xsem2(i))
            vector.tensor_tensor(out=y_buf[i % 3][:], in0=ytmp[:], in1=x_t[i % 4][:],
                                 op=ALU.add).then_inc(sem["v_y"], 1)
            vector.bn_stats(out=bn12[:, 0:6], in_=y_buf[i % 3][:, 0:512])
            vguard(vector, vector.bn_stats(out=bn12[:, 6:12], in_=y_buf[i % 3][:, 512:1024]))
            vector.bn_aggr(out=mvL[:], in_=bn12[:]).then_inc(sem["v_rL"], 1)
            vector.wait_ge(sem["s_invL"], i + 1)
            if i >= 2:
                vector.wait_ge(sem["t_lT"], 2 * (i - 2) + 2)
            vector.tensor_scalar(out=ln2tok[i % 2][:], in0=y_buf[i % 3][:],
                                 scalar1=mvL[:, 0:1], scalar2=invL[:],
                                 op0=ALU.subtract, op1=ALU.mult
                                 ).then_inc(sem["v_ln2e"], 1)
            if i >= 1:
                vector.bn_stats(out=bn12[:, 0:6], in_=ffn_f[:, 0:512])
                vguard(vector, vector.bn_stats(out=bn12[:, 6:12], in_=ffn_f[:, 512:1024]))
                vector.bn_aggr(out=mvF[:], in_=bn12[:]).then_inc(sem["v_rF"], 1)
                vector.wait_ge(sem["s_invF"], i)
                vector.tensor_scalar(out=ytmp[:], in0=ffn_f[:], scalar1=mvF[:, 0:1],
                                     scalar2=invF[:], op0=ALU.subtract, op1=ALU.mult)
                if i - 1 >= 2:
                    vector.wait_ge(*stsem(i - 3))
                vector.tensor_tensor(out=out_b[(i - 1) % 2][:], in0=ytmp[:],
                                     in1=y_buf[(i - 1) % 3][:], op=ALU.add
                                     ).then_inc(sem["v_out"], 1)
        vector.wait_ge(sem["t_ffn"], NT)
        vector.tensor_tensor(out=ffn_f[:], in0=psG[:], in1=bf_bc[:],
                             op=ALU.add).then_inc(sem["v_ffnev"], 1)
        vector.bn_stats(out=bn12[:, 0:6], in_=ffn_f[:, 0:512])
        vguard(vector, vector.bn_stats(out=bn12[:, 6:12], in_=ffn_f[:, 512:1024]))
        vector.bn_aggr(out=mvF[:], in_=bn12[:]).then_inc(sem["v_rF"], 1)
        vector.wait_ge(sem["s_invF"], NT)
        vector.tensor_scalar(out=ytmp[:], in0=ffn_f[:], scalar1=mvF[:, 0:1],
                             scalar2=invF[:], op0=ALU.subtract, op1=ALU.mult)
        vector.wait_ge(*stsem(NT - 3))
        vector.tensor_tensor(out=out_b[(NT - 1) % 2][:], in0=ytmp[:],
                             in1=y_buf[(NT - 1) % 3][:], op=ALU.add
                             ).then_inc(sem["v_out"], 1)

    @block.scalar
    def _(scalar):
        for t in (1, 3):
            scalar.dma_start(out=x_t[t % 4][:], in_=x_hbm[t * 128:(t + 1) * 128, :]
                             ).then_inc(xsem(t)[0], 16)
        scalar.wait_ge(sem["t_bc"], 1)
        scalar.dma_start(out=w_f32[0][:], in_=wq_hbm[0]).then_inc(sem["ld_w0"], 16)
        scalar.wait_ge(sem["ld_c"], 16 * 5)
        var16 = mv_all[:].rearrange("p (t two) -> p two t", two=2)[:, 1, :]

        def chain_s(g):
            c2 = slice(2 * g, 2 * g + 2)
            scalar.wait_ge(sem["v_r0"], 2 * g + 2)
            sguard(scalar, scalar.activation(out=lns16[:, c2], in_=var16[:, c2],
                                             func=AF.Ln, bias=epsc[:]))
            scalar.activation(out=inv16[:, c2], in_=lns16[:, c2], func=AF.Exp, scale=-0.5)
            sguard(scalar, scalar.activation(out=lns16[:, c2], in_=var16[:, c2],
                                             func=AF.Ln, scale=1024.0))
            scalar.activation(out=sa16[:, c2], in_=lns16[:, c2], func=AF.Exp, scale=0.5
                              ).then_inc(sem["s_inv0"], 1)

        def emit_s(t):
            scalar.wait_ge(sem["v_k"], (t - 1) // 2 + 1)
            if t >= 2:
                scalar.wait_ge(sem["t_scT"], 2 * t - 2)
            sguard(scalar, scalar.add(out=x_t[t % 4][:], in_=x_t[t % 4][:],
                                      add=nm16[:, t:t + 1]))
            scalar.mul(out=sctok[t % 2][:], in_=x_t[t % 4][:], mul=k16[:, t:t + 1]
                       ).then_inc(sem["s_emit"], 1)

        def vevac(tc):
            scalar.wait_ge(sem["t_q"], tc + 1)
            scalar.copy(out=vt[:, tc * D:(tc + 1) * D], in_=psv[tc % 2][:]
                        ).then_inc(sem["s_vt"], 1)

        scf4 = scf[:].rearrange("p (k t j) -> p k t j", k=KC, t=NT, j=128)

        def scf_pair(u):
            scalar.wait_ge(sem["t_scT"], 2 * u + 1)
            scalar.copy(out=scf4[:, 0:4, u, :],
                        in_=psq[0][:].rearrange("p (k j) -> p k j", k=4)
                        ).then_inc(sem["s_scF"], 1)
            scalar.wait_ge(sem["t_scT"], 2 * u + 2)
            scalar.copy(out=scf4[:, 4:8, u, :],
                        in_=psq[1][:].rearrange("p (k j) -> p k j", k=4)
                        ).then_inc(sem["s_scF"], 1)

        for t in range(NT):
            if t % 2 == 0:
                chain_s(t // 2)
            else:
                emit_s(t)
                if t + 4 <= NT - 1:
                    scalar.dma_start(out=x_t[t % 4][:],
                                     in_=x_hbm[(t + 4) * 128:(t + 5) * 128, :]
                                     ).then_inc(xsem(t + 4)[0], 16)
            if t >= 1:
                scf_pair(t - 1)
            if t >= 6:
                vevac(t - 6)
        scf_pair(NT - 1)
        for tc in range(NT - 6, NT):
            vevac(tc)
        scalar.wait_ge(sem["v_cast"], vcc(0))
        scalar.dma_start(out=w_f32[0][:], in_=wq_hbm[1]).then_inc(sem["ld_w1"], 16)
        for oc in range(KC):
            if oc + 2 < KC:
                scalar.wait_ge(sem["v_cast"], vcc(oc + 1))
                scalar.dma_start(out=w_f32[0][:], in_=wq_hbm[oc + 2]
                                 ).then_inc(wsem(oc + 2)[0], 16)
            elif oc + 2 < KC + 2:
                scalar.wait_ge(sem["v_cast"], vcc(oc + 1))
                scalar.dma_start(out=w_f32[0][:], in_=am_hbm[oc + 2 - KC]
                                 ).then_inc(wsem(oc + 2)[0], 16)
            for sl in range(4):
                scalar.wait_ge(sem["t_q"], 16 + 4 * oc + sl + 1)
                src_b = (psq[sl][:] if oc % 2 == 0
                         else psv[sl // 2][:, (sl % 2) * 512:(sl % 2) * 512 + 512])
                scalar.add(out=qf[:, oc * S + sl * 512: oc * S + (sl + 1) * 512],
                           in_=src_b, add=bq_t[:, oc:oc + 1]).then_inc(sem["s_qf"], 1)
        for k2 in range(KC):
            if k2 + 2 < KC:
                scalar.wait_ge(sem["v_cast"], vcc(9 + k2))
                scalar.dma_start(out=w_f32[0][:], in_=am_hbm[k2 + 2]
                                 ).then_inc(wsem(k2 + 10)[0], 16)
            for sl in range(4):
                scalar.wait_ge(sem["t_q"], 48 + 4 * k2 + sl + 1)
                src_b = (psq[sl][:] if k2 % 2 == 0
                         else psv[sl // 2][:, (sl % 2) * 512:(sl % 2) * 512 + 512])
                scalar.copy(out=tqf[:, k2 * S + sl * 512: k2 * S + (sl + 1) * 512],
                            in_=src_b).then_inc(sem["s_qf"], 1)
        sbank_s = [psA, psB, psC, psD]

        def sc_evac(j):
            for sl in range(4):
                g = 4 * j + sl
                scalar.wait_ge(sem["t_sc"], g + 1)
                scalar.mul(out=sc_sb[j % 2][:, sl * 512:(sl + 1) * 512],
                           in_=sbank_s[sl][:], mul=ISQ).then_inc(sem["v_scev"], 1)

        def s_exp(j):
            scalar.wait_ge(sem["v_max"], j + 1)
            if j >= 2:
                scalar.wait_ge(sem["t_pT"], 4 * (j - 2) + 4)
            scalar.activation(out=probs[j % 2][:], in_=sc_sb[j % 2][:], func=AF.Exp,
                              bias=negm_all[:, j:j + 1],
                              accum_out=sume_all[:, j:j + 1]).then_inc(sem["s_exp"], 1)

        def sc_evac_part(j, sls):
            for sl in sls:
                g = 4 * j + sl
                scalar.wait_ge(sem["t_sc"], g + 1)
                scalar.mul(out=sc_sb[j % 2][:, sl * 512:(sl + 1) * 512],
                           in_=sbank_s[sl][:], mul=ISQ).then_inc(sem["v_scev"], 1)

        def s_invL(j):
            scalar.wait_ge(sem["v_rL"], j + 1)
            sguard(scalar, scalar.activation(out=lns[:], in_=mvL[:, 1:2], func=AF.Ln, bias=epsc[:]))
            scalar.activation(out=invL[:], in_=lns[:], func=AF.Exp, scale=-0.5
                              ).then_inc(sem["s_invL"], 1)

        def s_invF(j):
            scalar.wait_ge(sem["v_rF"], j + 1)
            sguard(scalar, scalar.activation(out=lns[:], in_=mvF[:, 1:2], func=AF.Ln, bias=epsc[:]))
            scalar.activation(out=invF[:], in_=lns[:], func=AF.Exp, scale=-0.5
                              ).then_inc(sem["s_invF"], 1)

        def s_store(j):
            scalar.wait_ge(sem["v_out"], j + 1)
            if j >= 2:
                scalar.wait_ge(*stsem(j - 2))
            scalar.dma_start(out=y_hbm[j * 128: (j + 1) * 128, :],
                             in_=out_b[j % 2][:]).then_inc(stsem(j)[0], 16)

        sc_evac(0)
        s_exp(0)
        for i in range(NT):
            if i < NT - 1:
                scalar.wait_ge(sem["v_max"], i + 1)
                sc_evac_part(i + 1, (0, 1))
            if i >= 1:
                s_invL(i - 1)
            if i < NT - 1:
                sc_evac_part(i + 1, (2, 3))
            if i >= 1:
                scalar.wait_ge(sem["t_ctx"], i)
            for f in range(4):
                scalar.wait_ge(sem["t_pT"], 4 * i + f + 1)
                srcp = psE if f < 2 else psG
                scalar.copy(out=probsT[:, f * 512:(f + 1) * 512],
                            in_=srcp[:, (f % 2) * 512:(f % 2) * 512 + 512]
                            ).then_inc(sem["s_pT"], 1)
            if i >= 1:
                if i >= 2:
                    scalar.wait_ge(sem["t_ffn"], i - 1)
                for f in range(2):
                    scalar.wait_ge(sem["t_lT"], 2 * (i - 1) + f + 1)
                    scalar.copy(out=ln2F[(i - 1) % 2][:, f * 512:(f + 1) * 512],
                                in_=psAB[f][:]).then_inc(sem["s_lF"], 1)
            if i >= 2:
                s_invF(i - 2)
            if i < NT - 1:
                s_exp(i + 1)
            if i >= 2:
                s_store(i - 2)
            scalar.wait_ge(sem["v_rA"], i + 1)
            sguard(scalar, scalar.activation(out=lns[:], in_=mvA[:, 1:2], func=AF.Ln, bias=epsc[:]))
            scalar.activation(out=invA[:], in_=lns[:], func=AF.Exp, scale=-0.5
                              ).then_inc(sem["s_invA"], 1)
        s_invL(NT - 1)
        for f in range(2):
            scalar.wait_ge(sem["t_lT"], 2 * (NT - 1) + f + 1)
            scalar.copy(out=ln2F[(NT - 1) % 2][:, f * 512:(f + 1) * 512],
                        in_=psAB[f][:]).then_inc(sem["s_lF"], 1)
        s_invF(NT - 2)
        s_store(NT - 2)
        s_invF(NT - 1)
        s_store(NT - 1)
        scalar.wait_ge(sem["st0"], 16 * 8)
        scalar.wait_ge(sem["st1"], 16 * 8)

    if debug:
        @block.gpsimd
        def _(gpsimd):
            gpsimd.wait_ge(sem["st0"], 16 * 8)
            gpsimd.wait_ge(sem["st1"], 16 * 8)
            gpsimd.dma_start(out=dbg["d_qf"], in_=qf[:]).then_inc(sem["dbg"], 16)
            gpsimd.dma_start(out=dbg["d_tqf"], in_=tqf[:]).then_inc(sem["dbg"], 16)
            gpsimd.dma_start(out=dbg["d_vt"], in_=vt[:]).then_inc(sem["dbg"], 16)
            gpsimd.dma_start(out=dbg["d_wbig"], in_=wbig[:]).then_inc(sem["dbg"], 16)
            gpsimd.dma_start(out=dbg["d_norm"], in_=norm_all[:]).then_inc(sem["dbg"], 16)
            gpsimd.dma_start(out=dbg["d_sums"], in_=sume_all[:]).then_inc(sem["dbg"], 16)
            gpsimd.wait_ge(sem["dbg"], 16 * 6)

    es.close()
    return nc


def _tile8(w):
    return np.ascontiguousarray(
        np.transpose(w.reshape(8, 128, 8, 128), (2, 1, 0, 3)).reshape(8, 128, 1024))


def stage_inputs(L0_attn_hidden_states, L0_attn_Wq, L0_attn_bq, L0_attn_Wv, L0_attn_bv,
                 L0_attn_anomaly_matrix, L0_ffn_G1_R, L0_ffn_G2_R, L0_ffn_G1_I,
                 L0_ffn_G2_I, L0_ffn_bias_real, L0_ffn_bias_imag):
    f32 = np.float32
    x = np.asarray(L0_attn_hidden_states, dtype=f32)
    Wq = np.asarray(L0_attn_Wq, dtype=f32)
    Wv = np.asarray(L0_attn_Wv, dtype=f32)
    A = np.asarray(L0_attn_anomaly_matrix, dtype=f32)
    bq = np.asarray(L0_attn_bq, dtype=f32)
    bv = np.asarray(L0_attn_bv, dtype=f32)
    G1R = np.asarray(L0_ffn_G1_R, dtype=f32)
    G2R = np.asarray(L0_ffn_G2_R, dtype=f32)
    G1I = np.asarray(L0_ffn_G1_I, dtype=f32)
    G2I = np.asarray(L0_ffn_G2_I, dtype=f32)
    br = np.asarray(L0_ffn_bias_real, dtype=f32)
    bi = np.asarray(L0_ffn_bias_imag, dtype=f32)

    common = dict(
        wq_t=_tile8(np.ascontiguousarray(Wq.T)),
        amat_t=_tile8(A),
        wv_t=np.ascontiguousarray(Wv.T).reshape(8, 128, 1024),
        bq_t=np.ascontiguousarray(bq.reshape(8, 128).T),
        bv_row=bv.reshape(1, 1024),
        brbi_row=np.concatenate([br, bi]).reshape(1, 1024),
        ones_row=np.ones((1, 128), dtype=f32),
        ident=np.eye(128, dtype=ml_dtypes.bfloat16),
        e1=np.ascontiguousarray(
            np.stack([
                np.stack([np.repeat(np.repeat(np.ascontiguousarray(G1[0, :, c, :].T), 16, 0),
                                    16, 1) for c in range(4)])
                for G1 in (G1R, G1I)]).astype(f32)
            .reshape(2, 2, 2, 4, 128, 512)
            .transpose(0, 3, 1, 4, 2, 5)
            .reshape(2, 4, 2, 128, 1024)),
        e2=np.stack([
            np.stack([np.tile(np.ascontiguousarray(G2[c, :, 0, :].T), (8, 32))
                      for c in range(4)])
            for G2 in (G2R, G2I)]).astype(f32),
    )
    in_maps = [dict(common, x=np.ascontiguousarray(x[b])) for b in range(8)]
    return in_maps


def kernel(**inputs):
    in_maps = stage_inputs(**inputs)
    nc = build_nc()
    res = run_bass_kernel_spmd(nc, in_maps, core_ids=list(range(8)))
    return np.stack([res.results[c]["y_out"] for c in range(8)], axis=0)
